# version 04
# baseline (speedup 1.0000x reference)
import numpy as np
import concourse.bass as bass
import concourse.mybir as mybir

F32 = mybir.dt.float32
BF16 = mybir.dt.bfloat16
AF = mybir.ActivationFunctionType
ALU = mybir.AluOpType
AX = mybir.AxisListType

DM = 1024
DIN = 2048
EPS = 1e-5
KF = DM // 128
NU = DIN // 128
NCB = 2 * NU


class Sched:
    ENGINES = ("sync", "gpsimd", "vector", "scalar", "tensor")

    def __init__(self, nc, st):
        self.nc = nc
        self.ops = {e: [] for e in self.ENGINES}
        self.sems = {}
        self.cnt = {}
        self.guards = {}
        self.st = st
        self._pending = {e: {} for e in self.ENGINES}

    def sem(self, name):
        if name not in self.sems:
            self.sems[name] = self.st.enter_context(self.nc.semaphore(name))
            self.cnt[name] = 0
        return self.sems[name]

    def wait(self, eng, sem_name, val):
        if val <= 0:
            return
        p = self._pending[eng]
        p[sem_name] = max(p.get(sem_name, 0), val)

    def wait_now(self, eng, sem_name):
        self.wait(eng, sem_name, self.cnt[sem_name])

    def guard_wait(self, eng, key):
        for sem_name, tgt in self.guards.pop(key, []):
            if tgt is None:
                tgt = self.cnt[sem_name]
            self.wait(eng, sem_name, tgt)

    def guard_add(self, key, sem_name, tgt):
        self.guards.setdefault(key, []).append((sem_name, tgt))

    def emit(self, eng, fn, inc=None, inc_by=1, drain=False):
        pend = self._pending[eng]
        self._pending[eng] = {}
        val = None
        if inc is not None:
            self.sem(inc)
            self.cnt[inc] += inc_by
            val = self.cnt[inc]

        def op(e):
            if drain:
                e.drain()
            for s, v in pend.items():
                e.wait_ge(self.sems[s], v)
            ins = fn(e)
            if inc is not None:
                assert ins is not None, "inc requires a real instruction"
                ins.then_inc(self.sems[inc], inc_by)
        self.ops[eng].append(op)
        return val

    def dma(self, eng, out, in_, sem_name):
        return self.emit(eng, lambda e: e.dma_start(out=out, in_=in_), inc=sem_name, inc_by=16)

    def final_waits(self, eng="sync"):
        for s, v in sorted(self.cnt.items()):
            if s.startswith("o_") and v > 0:
                self.wait(eng, s, v)
        self.emit(eng, lambda e: None)

    def run(self):
        blk = self.st.enter_context(self.nc.Block())

        def mk(eng_name):
            def body(e):
                for op in self.ops[eng_name]:
                    op(e)
            return body
        blk.sync(mk("sync"))
        blk.gpsimd(mk("gpsimd"))
        blk.vector(mk("vector"))
        blk.scalar(mk("scalar"))
        blk.tensor(mk("tensor"))


def build_l1(T, CH=512):
    OV = CH
    T_PROC = OV + T
    assert T_PROC % CH == 0 and CH % 128 == 0
    NC = T_PROC // CH
    NTB = CH // 128

    nc = bass.Bass("TRN2", target_bir_lowering=False)
    x_in = nc.dram_tensor("x_in", [T_PROC, DM], F32, kind="ExternalInput").ap()
    w_in = nc.dram_tensor("w_in", [DM, 2 * DIN], F32, kind="ExternalInput").ap()
    cw_in = nc.dram_tensor("cw_in", [DIN, 4], F32, kind="ExternalInput").ap()
    cb_in = nc.dram_tensor("cb_in", [DIN], F32, kind="ExternalInput").ap()
    xpw_in = nc.dram_tensor("xpw_in", [DIN, 96], F32, kind="ExternalInput").ap()
    lng_in = nc.dram_tensor("lng_in", [DM], F32, kind="ExternalInput").ap()
    lnb_in = nc.dram_tensor("lnb_in", [DM], F32, kind="ExternalInput").ap()
    id_in = nc.dram_tensor("id_in", [128, 128], BF16, kind="ExternalInput").ap()

    ut_o = nc.dram_tensor("ut_o", [DIN, T], BF16, kind="ExternalOutput").ap()
    zt_o = nc.dram_tensor("zt_o", [DIN, T], BF16, kind="ExternalOutput").ap()
    xn_o = nc.dram_tensor("xn_o", [T, DM], BF16, kind="ExternalOutput").ap()
    dbc_o = nc.dram_tensor("dbc_o", [96, T], F32, kind="ExternalOutput").ap()
    mrs_o = nc.dram_tensor("mrs_o", [128, 2 * NC * NTB], F32, kind="ExternalOutput").ap()

    from contextlib import ExitStack
    with ExitStack() as st:
        def sb(name, shape, dt):
            return st.enter_context(nc.sbuf_tensor(name, shape, dt))

        def psum(name, shape, dt):
            return st.enter_context(nc.psum_tensor(name, shape, dt))

        w_bf = [sb(f"w_bf{k}", [128, 2 * DIN], BF16) for k in range(KF)]
        xpw_bf = [sb(f"xpw{k}", [128, 96], BF16) for k in range(NU)]
        cw_sb = sb("cw_sb", [128, 4 * NU], F32)
        cb_sb = sb("cb_sb", [128, NU], F32)
        g_bc = sb("g_bc", [128, DM], BF16)
        b_bc = sb("b_bc", [128, DM], BF16)
        lng_row = sb("lng_row", [1, DM], F32)
        lnb_row = sb("lnb_row", [1, DM], F32)
        id_bf = sb("id_bf", [128, 128], BF16)
        x_t = [sb(f"x_t{i}", [128, DM * NTB], F32) for i in range(2)]
        sq_scr = sb("sq_scr", [128, DM], F32)
        xn_scr = sb("xn_scr", [128, DM], F32)
        xn_bf = [sb(f"xn_bf{i}", [128, DM], BF16) for i in range(2)]
        xn2a = sb("xn2a", [128, DM], F32)
        xn2g = sb("xn2g", [128, DM], F32)
        xn2_bf = [sb(f"xn2_bf{i}", [128, DM], BF16) for i in range(2)]
        xn2t = [[sb(f"xn2t{i}_{k}", [128, CH], BF16) for k in range(KF)] for i in range(2)]
        u_scr = [sb(f"u_scr{i}", [128, CH + 3], F32) for i in range(2)]
        p_scr = [sb(f"p_scr{i}", [128, CH], F32) for i in range(2)]
        tails = sb("tails", [128, 3 * NU], F32)
        u_out = [[sb(f"u_out{i}_{u}", [128, CH], BF16) for u in range(NU)] for i in range(2)]
        z_bf = [sb(f"z_bf{i}", [128, CH], BF16) for i in range(4)]
        dbc_sb = [sb(f"dbc_sb{i}", [96, CH], F32) for i in range(2)]
        mrs_sb = sb("mrs_sb", [128, 2 * NC * NTB], F32)
        S = {}
        for nme in ["r1", "ss", "m1", "v1", "msq", "var", "lnv", "rstd1",
                    "v2a", "v2b", "v2c", "l2", "rstd2", "nmr", "sc12", "nms"]:
            S[nme] = sb(nme, [128, 1], F32)
        consts = sb("consts", [128, 4], F32)
        C_RDM = consts[:, 0:1]
        C_EPS = consts[:, 1:2]
        C_NEG1 = consts[:, 2:3]
        C_NEGH = consts[:, 3:4]
        nc.const_aps.register(F32, EPS, C_EPS)
        nc.const_aps.register(F32, -0.5, C_NEGH)
        nc.const_aps.register(F32, -1.0, C_NEG1)
        nc.const_aps.register(F32, 1.0 / DM, C_RDM)

        ps_tp = [psum(f"ps_tp{i}", [128, DM], BF16) for i in range(2)]
        ps_io = [psum(f"ps_io{i}", [128, CH], F32) for i in range(2)]
        ps_xp = psum("ps_xp", [96, CH], F32)

        sch = Sched(nc, st)
        for s in ["x0", "x1", "w", "wc", "v", "s", "t", "mm", "ec", "si", "cv", "db",
                  "o_u", "o_db", "o_z", "o_xn", "o_mrs"]:
            sch.sem(s)

        E = sch.emit
        W = sch.wait

        for k in range(KF):
            sch.dma("gpsimd", w_bf[k][:], w_in[k * 128:(k + 1) * 128, :], "w")
        for k in range(NU):
            sch.dma("gpsimd", xpw_bf[k][:], xpw_in[k * 128:(k + 1) * 128, :], "w")
        for ub in range(NU):
            sch.dma("gpsimd", cw_sb[:, 4 * ub:4 * ub + 4], cw_in[ub * 128:(ub + 1) * 128, :], "w")
        def _cb_load(e):
            return e.dma_start(out=cb_sb[:, :], in_=cb_in.rearrange("(a p) -> p a", p=128),
                               allow_slow_non_contiguous=True)
        sch.emit("gpsimd", _cb_load, inc="w", inc_by=16)
        sch.dma("gpsimd", lng_row[:], lng_in[None, :], "w")
        sch.dma("gpsimd", lnb_row[:], lnb_in[None, :], "w")
        sch.dma("gpsimd", id_bf[:], id_in[:, :], "w")
        sch.emit("gpsimd", lambda e: e.memset(consts[:, 0:1], 1.0 / DM), inc="wc")
        sch.emit("gpsimd", lambda e: e.memset(consts[:, 1:2], EPS), inc="wc")
        sch.emit("gpsimd", lambda e: e.memset(consts[:, 2:3], -1.0), inc="wc")
        sch.emit("gpsimd", lambda e: e.memset(consts[:, 3:4], -0.5), inc="wc")
        sch.emit("gpsimd", lambda e: e.memset(tails[:, :], 0.0), inc="wc")
        sch.dma("gpsimd", g_bc[:], lng_in[None, :].partition_broadcast(128), "w")
        sch.dma("gpsimd", b_bc[:], lnb_in[None, :].partition_broadcast(128), "w")
        W_ALL = sch.cnt["w"]
        W("vector", "w", W_ALL)
        W("vector", "wc", sch.cnt["wc"])
        W("scalar", "wc", sch.cnt["wc"])
        W("tensor", "wc", sch.cnt["wc"])
        V_BC = sch.cnt["v"]
        W("scalar", "w", W_ALL)
        W("tensor", "w", W_ALL)

        u_dma_idx = 0
        z_dma_idx = 0
        xn_dma_idx = 0
        db_dma_idx = 0

        def tb_work(c, tb):
            nonlocal xn_dma_idx
            g = c * NTB + tb
            xtb = x_t[c % 2][:, tb * DM:(tb + 1) * DM]
            xsem = f"x{c % 2}"
            xw = 16 * NTB * (c // 2 + 1)
            W("vector", xsem, xw)
            E("vector", lambda e: e.reduce_sum(S["r1"][:], xtb, axis=AX.X))
            W("scalar", xsem, xw)
            E("scalar", lambda e: e.activation(out=sq_scr[:], in_=xtb, func=AF.Square,
                                               accum_out=S["ss"][:]), inc="s")
            W("vector", "s", sch.cnt["s"])
            E("vector", lambda e: e.tensor_scalar(out=S["m1"][:], in0=S["r1"][:], scalar1=C_RDM, scalar2=None, op0=ALU.mult), drain=True)
            E("vector", lambda e: e.tensor_scalar(out=S["v1"][:], in0=S["ss"][:], scalar1=C_RDM, scalar2=None, op0=ALU.mult))
            E("vector", lambda e: e.scalar_tensor_tensor(out=S["var"][:], in0=S["m1"][:], scalar=S["m1"][:, 0:1], in1=S["v1"][:], op0=ALU.mult, op1=ALU.subtract), inc="v", drain=True)
            W("scalar", "v", sch.cnt["v"])
            E("scalar", lambda e: e.activation(out=S["lnv"][:], in_=S["var"][:], func=AF.Ln, scale=-1.0, bias=EPS))
            E("scalar", lambda e: e.activation(out=S["rstd1"][:], in_=S["lnv"][:], func=AF.Exp, scale=-0.5), inc="s", drain=True)
            W("vector", "s", sch.cnt["s"])
            E("vector", lambda e: e.scalar_tensor_tensor(out=S["v2b"][:], in0=S["rstd1"][:], scalar=S["rstd1"][:, 0:1], in1=S["var"][:], op0=ALU.mult, op1=ALU.mult), inc="v", drain=True)
            W("scalar", "v", sch.cnt["v"])
            E("scalar", lambda e: e.activation(out=S["l2"][:], in_=S["v2b"][:], func=AF.Ln, scale=-1.0, bias=EPS))
            E("scalar", lambda e: e.activation(out=S["rstd2"][:], in_=S["l2"][:], func=AF.Exp, scale=-0.5), inc="s", drain=True)
            W("vector", "s", sch.cnt["s"])
            E("vector", lambda e: e.scalar_tensor_tensor(out=S["nmr"][:], in0=S["m1"][:], scalar=C_NEG1[:, 0:1], in1=S["rstd1"][:], op0=ALU.mult, op1=ALU.mult), drain=True)
            E("vector", lambda e: e.tensor_tensor(out=S["sc12"][:], in0=S["rstd1"][:], in1=S["rstd2"][:], op=ALU.mult))
            E("vector", lambda e: e.scalar_tensor_tensor(out=S["nms"][:], in0=S["m1"][:], scalar=C_NEG1[:, 0:1], in1=S["sc12"][:], op0=ALU.mult, op1=ALU.mult), drain=True)
            E("vector", lambda e: e.tensor_copy(mrs_sb[:, 2 * g:2 * g + 1], S["m1"][:]))
            E("vector", lambda e: e.tensor_copy(mrs_sb[:, 2 * g + 1:2 * g + 2], S["rstd1"][:]), inc="v")
            W("scalar", "v", sch.cnt["v"])
            sch.guard_wait("scalar", ("xn_bf", g % 2))
            E("scalar", lambda e: e.activation(out=xn_scr[:], in_=xtb, func=AF.Identity,
                                               scale=S["rstd1"][:, 0:1], bias=S["nmr"][:, 0:1]))
            E("scalar", lambda e: e.activation(out=xn2a[:], in_=xtb, func=AF.Identity,
                                               scale=S["sc12"][:, 0:1], bias=S["nms"][:, 0:1]))
            E("scalar", lambda e: e.activation(out=xn_bf[g % 2][:], in_=xn_scr[:], func=AF.Copy), inc="s", drain=True)
            s_xn = sch.cnt["s"]
            if c >= 1:
                og = (c - 1) * NTB + tb
                W("gpsimd", "s", s_xn)
                sch.dma("gpsimd", xn_o[og * 128:(og + 1) * 128, :], xn_bf[g % 2][:, :], "o_xn")
                xn_dma_idx += 1
                sch.guard_add(("xn_bf", g % 2), "o_xn", None)
            W("vector", "s", s_xn)
            W("vector", "v", V_BC)
            sch.guard_wait("vector", ("xn2_bf", g % 2))
            E("vector", lambda e: e.tensor_tensor(out=xn2g[:], in0=xn2a[:], in1=g_bc[:], op=ALU.mult), drain=True)
            E("vector", lambda e: e.tensor_tensor(out=xn2_bf[g % 2][:], in0=xn2g[:], in1=b_bc[:], op=ALU.add), inc="v", drain=True)
            W("tensor", "v", sch.cnt["v"])
            sch.guard_wait("tensor", ("ps_tp", g % 2))
            for k in range(KF):
                def tp(e, k=k):
                    return e.transpose(out=ps_tp[g % 2][:, k * 128:(k + 1) * 128],
                                       in_=xn2_bf[g % 2][:, k * 128:(k + 1) * 128], identity=id_bf[:])
                E("tensor", tp, inc=("t" if k == KF - 1 else None))
            sch.guard_add(("xn2_bf", g % 2), "t", sch.cnt["t"])
            W("scalar", "t", sch.cnt["t"])
            sch.guard_wait("scalar", ("xn2t", c % 2, tb))
            for k in range(KF):
                def ev(e, k=k):
                    return e.activation(out=xn2t[c % 2][k][:, tb * 128:(tb + 1) * 128],
                                        in_=ps_tp[g % 2][:, k * 128:(k + 1) * 128], func=AF.Copy)
                E("scalar", ev, inc=("s" if k == KF - 1 else None))
            sch.guard_add(("ps_tp", g % 2), "s", sch.cnt["s"])
            return sch.cnt["s"]

        def chunk_mm(c, evac_milestones):
            nonlocal u_dma_idx, z_dma_idx, db_dma_idx
            W("tensor", "s", evac_milestones[c])
            zj_local = []
            for cb in range(NCB):
                gcb = c * NCB + cb
                sch.guard_wait("tensor", ("ps_io", gcb % 2))
                for k in range(KF):
                    def mm(e, k=k, cb=cb, gcb=gcb):
                        return e.matmul(out=ps_io[gcb % 2][:, :],
                                        lhsT=w_bf[k][:, cb * 128:(cb + 1) * 128],
                                        rhs=xn2t[c % 2][k][:, :],
                                        start=(k == 0), stop=(k == KF - 1))
                    E("tensor", mm, inc=("mm" if k == KF - 1 else None))
                mm_done = sch.cnt["mm"]
                W("scalar", "mm", mm_done)
                if cb < NU:
                    ub = cb
                    sch.guard_wait("scalar", ("u_scr", ub % 2))
                    E("scalar", lambda e, ub=ub, gcb=gcb: e.activation(
                        out=u_scr[ub % 2][:, 3:CH + 3], in_=ps_io[gcb % 2][:, :], func=AF.Copy), inc="ec")
                    sch.guard_add(("ps_io", gcb % 2), "ec", sch.cnt["ec"])
                    W("vector", "ec", sch.cnt["ec"])
                    sch.guard_wait("vector", ("p_scr", ub % 2))
                    w4 = cw_sb[:, 4 * ub:4 * ub + 4]
                    us = u_scr[ub % 2]
                    pp = p_scr[ub % 2]
                    E("vector", lambda e, ub=ub, us=us: e.tensor_copy(us[:, 0:3], tails[:, 3 * ub:3 * ub + 3]), drain=True)
                    E("vector", lambda e, us=us, pp=pp, w4=w4: e.tensor_scalar(
                        out=pp[:], in0=us[:, 0:CH], scalar1=w4[:, 0:1], scalar2=None, op0=ALU.mult), drain=True)
                    for j in (1, 2, 3):
                        E("vector", lambda e, us=us, pp=pp, w4=w4, j=j: e.scalar_tensor_tensor(
                            out=pp[:], in0=us[:, j:CH + j], scalar=w4[:, j:j + 1], in1=pp[:],
                            op0=ALU.mult, op1=ALU.add), drain=True)
                    E("vector", lambda e, ub=ub, us=us: e.tensor_copy(
                        tails[:, 3 * ub:3 * ub + 3], us[:, CH:CH + 3]), inc="cv", drain=True)
                    sch.guard_add(("u_scr", ub % 2), "cv", sch.cnt["cv"])
                    W("scalar", "cv", sch.cnt["cv"])
                    sch.guard_wait("scalar", ("u_out", c % 2, ub))
                    E("scalar", lambda e, c=c, ub=ub: e.activation(
                        out=u_out[c % 2][ub][:, :], in_=p_scr[ub % 2][:, :], func=AF.Silu,
                        bias=cb_sb[:, ub:ub + 1]), inc="si")
                    sch.guard_add(("p_scr", ub % 2), "si", sch.cnt["si"])
                    if c >= 1:
                        oc = c - 1
                        W("sync", "si", sch.cnt["si"])
                        sch.dma("sync", ut_o[ub * 128:(ub + 1) * 128, oc * CH:(oc + 1) * CH],
                                u_out[c % 2][ub][:, :], "o_u")
                        u_dma_idx += 1
                        sch.guard_add(("u_out", c % 2, ub), "o_u", None)
                else:
                    zb = z_dma_idx % 4
                    sch.guard_wait("scalar", ("z_bf", zb))
                    E("scalar", lambda e, zb=zb, gcb=gcb: e.activation(
                        out=z_bf[zb][:, :], in_=ps_io[gcb % 2][:, :], func=AF.Copy), inc="ec")
                    sch.guard_add(("ps_io", gcb % 2), "ec", sch.cnt["ec"])
                    if c >= 1:
                        oc = c - 1
                        ub = cb - NU
                        W("gpsimd", "ec", sch.cnt["ec"])
                        sch.dma("gpsimd", zt_o[ub * 128:(ub + 1) * 128, oc * CH:(oc + 1) * CH],
                                z_bf[zb][:, :], "o_z")
                        sch.guard_add(("z_bf", zb), "o_z", None)
                    z_dma_idx += 1
            W("tensor", "si", sch.cnt["si"])
            sch.guard_wait("tensor", ("ps_xp",))
            for k in range(NU):
                def mmx(e, k=k):
                    return e.matmul(out=ps_xp[:, :], lhsT=xpw_bf[k][:, :],
                                    rhs=u_out[c % 2][k][:, :],
                                    start=(k == 0), stop=(k == NU - 1))
                E("tensor", mmx, inc=("mm" if k == NU - 1 else None))
            for k in range(NU):
                sch.guard_add(("u_out", c % 2, k), "mm", sch.cnt["mm"])
            W("scalar", "mm", sch.cnt["mm"])
            sch.guard_wait("scalar", ("dbc_sb", c % 2))
            E("scalar", lambda e, c=c: e.activation(out=dbc_sb[c % 2][:, :], in_=ps_xp[:, :],
                                                    func=AF.Copy), inc="db")
            sch.guard_add(("ps_xp",), "db", sch.cnt["db"])
            if c >= 1:
                oc = c - 1
                W("sync", "db", sch.cnt["db"])
                sch.dma("sync", dbc_o[:, oc * CH:(oc + 1) * CH], dbc_sb[c % 2][:, :], "o_db")
                db_dma_idx += 1
                sch.guard_add(("dbc_sb", c % 2), "o_db", None)

        evac_ms = {}
        for c in range(NC):
            for tb in range(NTB):
                g = c * NTB + tb
                if c >= 2:
                    sch.guard_wait("sync", ("x_t", c % 2, tb))
                sch.dma("sync", x_t[c % 2][:, tb * DM:(tb + 1) * DM],
                        x_in[g * 128:(g + 1) * 128, :], f"x{c % 2}")
            for tb in range(NTB):
                ms = tb_work(c, tb)
                evac_ms[c] = ms
                g = c * NTB + tb
                sch.guard_add(("x_t", c % 2, tb), "s", sch.cnt["s"])
            chunk_mm(c, evac_ms)
        W("sync", "v", sch.cnt["v"])
        sch.dma("sync", mrs_o[:, :], mrs_sb[:, :], "o_mrs")
        sch.final_waits("sync")
        sch.run()
    return nc



N_ST = 16
N_EXP = 32


def build_l2(TB, TMOE, TCH=512, TH=2048, MC=256, DG=4):
    DH = 1024
    NDT = DH // 128
    NCHUNK = TB // TCH
    NTH = TMOE // TH
    NMC = TH // MC
    EG = 2
    NEG = N_EXP // EG
    NDG = NDT // DG
    MTB = MC // 128
    NSP = TB // 512
    assert TMOE % TH == 0 and TH % MC == 0 and MC % 128 == 0 and TB % TCH == 0
    assert TCH % 512 == 0 or TCH == 512

    nc = bass.Bass("TRN2", target_bir_lowering=False)
    ut_in = nc.dram_tensor("ut_in", [DH, TB], BF16, kind="ExternalInput").ap()
    zt_in = nc.dram_tensor("zt_in", [DH, TB], BF16, kind="ExternalInput").ap()
    dbct_in = nc.dram_tensor("dbct_in", [96, TB], F32, kind="ExternalInput").ap()
    xn_in = nc.dram_tensor("xn_in", [TMOE, DM], BF16, kind="ExternalInput").ap()
    alog_in = nc.dram_tensor("alog_in", [DH, N_ST], F32, kind="ExternalInput").ap()
    dsl_in = nc.dram_tensor("dsl_in", [DH], F32, kind="ExternalInput").ap()
    dtw_in = nc.dram_tensor("dtw_in", [64, DH], F32, kind="ExternalInput").ap()
    dtb_in = nc.dram_tensor("dtb_in", [DH], F32, kind="ExternalInput").ap()
    rw_in = nc.dram_tensor("rw_in", [N_EXP, DM], F32, kind="ExternalInput").ap()
    ew_in = nc.dram_tensor("ew_in", [N_EXP, DM, DM], F32, kind="ExternalInput").ap()
    id_in = nc.dram_tensor("id_in", [128, 128], BF16, kind="ExternalInput").ap()

    yzt_o = nc.dram_tensor("yzt_o", [DH, TB], BF16, kind="ExternalOutput").ap()
    g_o = nc.dram_tensor("g_o", [TMOE, DM], BF16, kind="ExternalOutput").ap()

    delta_dr = nc.dram_tensor("delta_dr", [DH, TB], F32)
    wt_dr = nc.dram_tensor("wt_dr", [N_EXP, TMOE], BF16)

    from contextlib import ExitStack
    with ExitStack() as st:
        def sb(name, shape, dt):
            return st.enter_context(nc.sbuf_tensor(name, shape, dt))

        def psum(name, shape, dt):
            return st.enter_context(nc.psum_tensor(name, shape, dt))

        a_t = [sb(f"a_t{j}", [128, N_ST], F32) for j in range(NDT)]
        alog_scr = sb("alog_scr", [128, N_ST], F32)
        state = [sb(f"state{j}", [128, N_ST], F32) for j in range(NDT)]
        dcol = sb("dcol", [128, NDT], F32)
        dtbcol = sb("dtbcol", [128, NDT], F32)
        dtw_bf = sb("dtw_bf", [64, DH], BF16)
        dtT_st = [sb(f"dtT_st{i}", [64, 512], BF16) for i in range(2)]
        rw_bf = sb("rw_bf", [N_EXP, DM], BF16)
        rwt = [sb(f"rwt{k}", [128, N_EXP], BF16) for k in range(KF)]
        id_bf = sb("id_bf2", [128, 128], BF16)
        consts = sb("consts2", [128, 2], F32)
        C_NEG1 = consts[:, 0:1]
        C_BIGN = consts[:, 1:2]
        nc.const_aps.register(F32, -1.0, C_NEG1)

        delta_t = [[sb(f"del{i}_{j}", [128, TCH], F32) for j in range(DG)] for i in range(2)]
        u_t = [[sb(f"u{i}_{j}", [128, TCH], BF16) for j in range(DG)] for i in range(2)]
        du_t = [sb(f"du_{j}", [128, TCH], BF16) for j in range(DG)]
        acc_a = [sb(f"acca{j}", [128, TCH], BF16) for j in range(DG)]
        acc_b = [sb(f"accb{j}", [128, TCH], BF16) for j in range(DG)]
        a_scr = [sb(f"a_scr{i}", [128, TCH], F32) for i in range(2)]
        b_scr = sb("b_scr", [128, TCH], BF16)
        h_scr = sb("h_scr", [128, TCH], BF16)
        p_scr2 = sb("p_scr2", [128, TCH], BF16)
        y_scr = sb("y_scr", [128, TCH], F32)
        y2_scr = sb("y2_scr", [128, TCH], F32)
        bbc = [sb(f"bbc{i}", [128, TCH], BF16) for i in range(2)]
        cbc = [sb(f"cbc{i}", [128, TCH], BF16) for i in range(2)]
        z_t = [sb(f"z_t{i}", [128, TCH], BF16) for i in range(2)]
        sz_t = sb("sz_t", [128, TCH], BF16)
        yz_t = [sb(f"yz_t{i}", [128, TCH], BF16) for i in range(2)]

        xn_tb = [sb(f"xn_tb{i}", [128, DM], BF16) for i in range(2)]
        xnt = [sb(f"xnt{k}", [128, TH], BF16) for k in range(KF)]
        gacc = [sb(f"gacc{t}", [128, DM], BF16) for t in range(TH // 128)]
        we = [[sb(f"we{i}_{e}_{k}", [128, DM], BF16) for k in range(KF)]
              for i in range(2) for e in range(EG)]
        xw = [[sb(f"xw{i}_{k}", [128, MC], BF16) for k in range(KF)] for i in range(2)]
        wbc = [sb(f"wbc{i}", [128, MC], BF16) for i in range(2)]
        ev_scr = sb("ev_scr", [128, 512], BF16)
        g_sil = [sb(f"g_sil{i}", [128, DM], BF16) for i in range(2)]
        wt_tb = [sb(f"wt_tb{i}", [N_EXP, 128], BF16) for i in range(2)]
        wm_bf = sb("wm_bf", [128, N_EXP], BF16)
        Sr = {}
        for nme in ["lmax", "nlmax", "psr", "prcp", "m1v", "m2v"]:
            Sr[nme] = sb(nme, [128, 1], F32)
        pexp = sb("pexp", [128, N_EXP], F32)
        rw_t = sb("rw_t", [128, N_EXP], F32)
        eqm = sb("eqm", [128, N_EXP], F32)
        rw2 = sb("rw2", [128, N_EXP], F32)
        keep = sb("keep", [128, N_EXP], F32)
        sp_t3 = sb("sp_t3", [128, 512], F32)
        sp_t4 = sb("sp_t4", [128, 512], F32)
        del_sl = sb("del_sl", [128, 512], F32)

        ps_moe = [psum(f"ps_moe{i}", [128, 512], F32) for i in range(4)]
        ps_tp = psum("ps_tp_l2", [128, DM], BF16)
        ps_wt = psum("ps_wt_l2", [N_EXP, 128], BF16)
        ps_rt = ps_moe[2][:, 0:N_EXP]

        sch = Sched(nc, st)
        E = sch.emit
        W = sch.wait

        sch.dma("gpsimd", id_bf[:], id_in[:, :], "w")
        sch.dma("gpsimd", dtw_bf[:], dtw_in[:, :], "w")
        sch.dma("gpsimd", rw_bf[:], rw_in[:, :], "w")
        E("gpsimd", lambda e: e.dma_start(out=dcol[:, :], in_=dsl_in.rearrange("(a p) -> p a", p=128),
                                          allow_slow_non_contiguous=True), inc="w", inc_by=16)
        E("gpsimd", lambda e: e.dma_start(out=dtbcol[:, :], in_=dtb_in.rearrange("(a p) -> p a", p=128),
                                          allow_slow_non_contiguous=True), inc="w", inc_by=16)
        E("gpsimd", lambda e: e.memset(consts[:, 0:1], -1.0), inc="wc")
        E("gpsimd", lambda e: e.memset(consts[:, 1:2], -1e30), inc="wc")
        for j in range(NDT):
            E("gpsimd", lambda e, j=j: e.memset(state[j][:, :], 0.0), inc="wc")
        for j in range(NDT):
            sch.guard_wait("gpsimd", ("alog_scr",))
            sch.dma("gpsimd", alog_scr[:, :], alog_in[j * 128:(j + 1) * 128, :], "wa")
            W("scalar", "wa", sch.cnt["wa"])
            E("scalar", lambda e, j=j: e.activation(out=a_t[j][:, :], in_=alog_scr[:, :],
                                                    func=AF.Exp), inc="s", drain=True)
            sch.guard_add(("alog_scr",), "s", sch.cnt["s"])
            W("vector", "s", sch.cnt["s"])
            E("vector", lambda e, j=j: e.tensor_scalar(out=a_t[j][:, :], in0=a_t[j][:, :],
                                                       scalar1=C_NEG1, scalar2=None, op0=ALU.mult),
              inc="v", drain=True)
        V_A = sch.cnt["v"]
        W("scalar", "wc", sch.cnt["wc"])
        W("scalar", "w", sch.cnt["w"])
        W("vector", "wc", sch.cnt["wc"])
        W("vector", "w", sch.cnt["w"])
        W("tensor", "w", sch.cnt["w"])
        for k in range(KF):
            if k > 0:
                sch.guard_wait("tensor", ("ps_tp",))
            E("tensor", lambda e, k=k: e.transpose(out=ps_tp[:, 0:N_EXP],
                                                   in_=rw_bf[:, k * 128:(k + 1) * 128],
                                                   identity=id_bf[0:N_EXP, 0:N_EXP]), inc="t")
            W("scalar", "t", sch.cnt["t"])
            E("scalar", lambda e, k=k: e.activation(out=rwt[k][:, :], in_=ps_tp[:, 0:N_EXP],
                                                    func=AF.Copy), inc="s")
            sch.guard_add(("ps_tp",), "s", sch.cnt["s"])

        dsp = 0
        for j in range(NDT):
            for ts in range(NSP):
                pd = ps_moe[dsp % 2]
                stp = dsp % 2
                sch.guard_wait("gpsimd", ("dtT_st", stp))
                sch.dma("gpsimd", dtT_st[stp][:, :], dbct_in[0:64, ts * 512:(ts + 1) * 512],
                        f"dts{stp}")
                sch.guard_wait("tensor", ("ps_moe", dsp % 2))
                W("tensor", "v", V_A)
                W("tensor", f"dts{stp}", sch.cnt[f"dts{stp}"])
                E("tensor", lambda e, j=j, ts=ts, pd=pd, stp=stp: e.matmul(
                    out=pd[:, :], lhsT=dtw_bf[:, j * 128:(j + 1) * 128],
                    rhs=dtT_st[stp][:, :],
                    start=True, stop=True), inc="mm")
                sch.guard_add(("dtT_st", stp), "mm", sch.cnt["mm"])
                W("scalar", "mm", sch.cnt["mm"])
                sch.guard_wait("scalar", ("del_sl",))
                E("scalar", lambda e, j=j, pd=pd: e.activation(out=sp_t3[:, :], in_=pd[:, :],
                                                               func=AF.Abs, bias=dtbcol[:, j:j + 1]))
                E("scalar", lambda e: e.activation(out=sp_t3[:, :], in_=sp_t3[:, :],
                                                   func=AF.Exp, scale=-1.0), drain=True)
                E("scalar", lambda e: e.activation(out=sp_t3[:, :], in_=sp_t3[:, :],
                                                   func=AF.Ln, bias=1.0), drain=True)
                E("scalar", lambda e, j=j, pd=pd: e.activation(out=sp_t4[:, :], in_=pd[:, :],
                                                               func=AF.Relu, bias=dtbcol[:, j:j + 1]),
                  inc="s", drain=True)
                sch.guard_add(("ps_moe", dsp % 2), "s", sch.cnt["s"])
                W("vector", "s", sch.cnt["s"])
                E("vector", lambda e: e.tensor_tensor(out=del_sl[:, :], in0=sp_t3[:, :],
                                                      in1=sp_t4[:, :], op=ALU.add), inc="v", drain=True)
                W("sync", "v", sch.cnt["v"])
                sch.dma("sync", delta_dr[j * 128:(j + 1) * 128, ts * 512:(ts + 1) * 512],
                        del_sl[:, :], "dsw")
                sch.guard_add(("del_sl",), "dsw", None)
                dsp += 1
        DSW_ALL = sch.cnt["dsw"]

        uidx = 0
        scan_units = [(ci, dgi) for ci in range(NCHUNK) for dgi in range(NDG)]
        moe_units = [(th, eg, mc) for th in range(NTH) for eg in range(NEG) for mc in range(NMC)]
        n_scan = len(scan_units)
        n_moe = len(moe_units)

        def emit_router_preamble(th):
            for tb in range(TH // 128):
                gtb = th * (TH // 128) + tb
                par = gtb % 2
                sch.guard_wait("gpsimd", ("xn_tb", par))
                sch.dma("gpsimd", xn_tb[par][:, :],
                        xn_in[(th * TH + tb * 128):(th * TH + (tb + 1) * 128), :], f"xn{par}")
                W("tensor", f"xn{par}", sch.cnt[f"xn{par}"])
                sch.guard_wait("tensor", ("ps_tp",))
                for k in range(KF):
                    E("tensor", lambda e, k=k, par=par: e.transpose(
                        out=ps_tp[:, k * 128:(k + 1) * 128],
                        in_=xn_tb[par][:, k * 128:(k + 1) * 128], identity=id_bf[:]),
                      inc=("t" if k == KF - 1 else None))
                sch.guard_add(("xn_tb", par), "t", sch.cnt["t"])
                W("scalar", "t", sch.cnt["t"])
                sch.guard_wait("scalar", ("xnt", tb))
                for k in range(KF):
                    E("scalar", lambda e, k=k, tb=tb: e.activation(
                        out=xnt[k][:, tb * 128:(tb + 1) * 128],
                        in_=ps_tp[:, k * 128:(k + 1) * 128], func=AF.Copy),
                      inc=("s" if k == KF - 1 else None))
                sch.guard_add(("ps_tp",), "s", sch.cnt["s"])
                W("tensor", "s", sch.cnt["s"])
                sch.guard_wait("tensor", ("ps_moe", 2))
                for k in range(KF):
                    E("tensor", lambda e, k=k, tb=tb: e.matmul(
                        out=ps_rt[:, :], lhsT=xnt[k][:, tb * 128:(tb + 1) * 128],
                        rhs=rwt[k][:, :], start=(k == 0), stop=(k == KF - 1)),
                      inc=("t" if k == KF - 1 else None))
                T_RT = sch.cnt["t"]
                W("vector", "t", T_RT)
                E("vector", lambda e: e.reduce_max(Sr["lmax"][:], ps_rt[:, :], axis=AX.X), drain=True)
                E("vector", lambda e: e.tensor_scalar(out=Sr["nlmax"][:], in0=Sr["lmax"][:],
                                                      scalar1=C_NEG1, scalar2=None, op0=ALU.mult),
                  inc="v", drain=True)
                W("scalar", "v", sch.cnt["v"])
                E("scalar", lambda e: e.activation(out=pexp[:, :], in_=ps_rt[:, :], func=AF.Exp,
                                                   bias=Sr["nlmax"][:, 0:1]), inc="s", drain=True)
                sch.guard_add(("ps_moe", 2), "s", sch.cnt["s"])
                W("vector", "s", sch.cnt["s"])
                E("vector", lambda e: e.reduce_sum(Sr["psr"][:], pexp[:, :], axis=AX.X), drain=True)
                E("vector", lambda e: e.reciprocal(Sr["prcp"][:], Sr["psr"][:]), drain=True)
                E("vector", lambda e: e.tensor_scalar(out=rw_t[:, :], in0=pexp[:, :],
                                                      scalar1=Sr["prcp"][:, 0:1], scalar2=None,
                                                      op0=ALU.mult), drain=True)
                E("vector", lambda e: e.reduce_max(Sr["m1v"][:], rw_t[:, :], axis=AX.X), drain=True)
                E("vector", lambda e: e.tensor_scalar(out=eqm[:, :], in0=rw_t[:, :],
                                                      scalar1=Sr["m1v"][:, 0:1], scalar2=None,
                                                      op0=ALU.is_equal), drain=True)
                E("vector", lambda e: e.scalar_tensor_tensor(out=rw2[:, :], in0=eqm[:, :],
                                                             scalar=C_BIGN[:, 0:1], in1=rw_t[:, :],
                                                             op0=ALU.mult, op1=ALU.add), drain=True)
                E("vector", lambda e: e.reduce_max(Sr["m2v"][:], rw2[:, :], axis=AX.X), drain=True)
                E("vector", lambda e: e.tensor_scalar(out=keep[:, :], in0=rw_t[:, :],
                                                      scalar1=Sr["m2v"][:, 0:1], scalar2=None,
                                                      op0=ALU.is_ge), drain=True)
                sch.guard_wait("vector", ("wm_bf",))
                E("vector", lambda e: e.tensor_tensor(out=wm_bf[:, :], in0=rw_t[:, :],
                                                      in1=keep[:, :], op=ALU.mult), inc="v", drain=True)
                W("tensor", "v", sch.cnt["v"])
                sch.guard_wait("tensor", ("ps_wt",))
                E("tensor", lambda e: e.transpose(out=ps_wt[:, :], in_=wm_bf[:, :],
                                                  identity=id_bf[:]), inc="t")
                sch.guard_add(("wm_bf",), "t", sch.cnt["t"])
                W("scalar", "t", sch.cnt["t"])
                wpar2 = tb % 2
                sch.guard_wait("scalar", ("wt_tb", wpar2))
                E("scalar", lambda e, wpar2=wpar2: e.activation(out=wt_tb[wpar2][:, :],
                                                                in_=ps_wt[:, :], func=AF.Copy), inc="s")
                sch.guard_add(("ps_wt",), "s", sch.cnt["s"])
                W("sync", "s", sch.cnt["s"])
                sch.dma("sync", wt_dr[:, th * TH + tb * 128: th * TH + (tb + 1) * 128],
                        wt_tb[wpar2][:, :], "wts")
                sch.guard_add(("wt_tb", wpar2), "wts", None)
            for t in range(TH // 128):
                sch.guard_wait("gpsimd", ("gacc", t))
                E("gpsimd", lambda e, t=t: e.memset(gacc[t][:, :], 0.0), inc="gz")

        def load_we_pair(eg, par, th):
            for el in range(EG):
                ex = eg * EG + el
                for k in range(KF):
                    sch.guard_wait("gpsimd", ("we", 2 * par + el, k))
                    sch.dma("gpsimd", we[2 * par + el][k][:, :],
                            ew_in[ex, k * 128:(k + 1) * 128, :], f"we{par}")

        def emit_moe_unit(th, eg, mc, sidx):
            par = sidx % 2
            wpar = eg % 2
            if mc == 0:
                nxt = eg + 1
                if nxt < NEG:
                    load_we_pair(nxt, (eg + 1) % 2, th)
                elif th + 1 < NTH:
                    load_we_pair(0, (eg + 1) % 2, th + 1)
            for el in range(EG):
                ex = eg * EG + el
                bpar = (2 * sidx + el) % 2
                sch.guard_wait("gpsimd", ("wbc", bpar))
                W("gpsimd", "wts", sch.cnt["wts"])
                sch.dma("gpsimd", wbc[bpar][:, :],
                        wt_dr[ex:ex + 1, th * TH + mc * MC: th * TH + (mc + 1) * MC].partition_broadcast(128),
                        f"wb{bpar}")
                W("vector", f"wb{bpar}", sch.cnt[f"wb{bpar}"])
                W("vector", "wts", sch.cnt["wts"])
                xpar = (2 * sidx + el) % 2
                for k in range(KF):
                    sch.guard_wait("vector", ("xw", xpar, k))
                    E("vector", lambda e, k=k, xpar=xpar, bpar=bpar: e.tensor_tensor(
                        out=xw[xpar][k][:, :], in0=xnt[k][:, mc * MC:(mc + 1) * MC],
                        in1=wbc[bpar][:, :], op=ALU.mult),
                      inc=("v" if k == KF - 1 else None), drain=(k == 0))
                sch.guard_add(("wbc", bpar), "v", sch.cnt["v"])
                V_MASK = sch.cnt["v"]
                W("tensor", "v", V_MASK)
                W("tensor", f"we{wpar}", sch.cnt[f"we{wpar}"])
                for tb in range(MTB):
                    for dmh in range(2):
                        pj = 2 * tb + dmh
                        if el == 0:
                            sch.guard_wait("tensor", ("ps_moe", pj))
                        for k in range(KF):
                            E("tensor", lambda e, k=k, tb=tb, dmh=dmh, el=el, pj=pj, xpar=xpar, wpar=wpar: e.matmul(
                                out=ps_moe[pj][:, :],
                                lhsT=xw[xpar][k][:, tb * 128:(tb + 1) * 128],
                                rhs=we[2 * wpar + el][k][:, dmh * 512:(dmh + 1) * 512],
                                start=(el == 0 and k == 0), stop=(el == EG - 1 and k == KF - 1)),
                              inc=("mm" if (el == EG - 1 and k == KF - 1) else None))
            MM_D = sch.cnt["mm"]
            if mc == NMC - 1:
                for el in range(EG):
                    for k in range(KF):
                        sch.guard_add(("we", 2 * wpar + el, k), "mm", MM_D)
            for xp in range(2):
                for k in range(KF):
                    sch.guard_add(("xw", xp, k), "mm", MM_D)
            for tb in range(MTB):
                gt = mc * MTB + tb
                for dmh in range(2):
                    pj = 2 * tb + dmh
                    W("scalar", "mm", MM_D)
                    sch.guard_wait("scalar", ("ev_scr",))
                    E("scalar", lambda e, pj=pj: e.activation(out=ev_scr[:, :], in_=ps_moe[pj][:, :],
                                                              func=AF.Copy), inc="s")
                    sch.guard_add(("ps_moe", pj), "s", sch.cnt["s"])
                    W("vector", "s", sch.cnt["s"])
                    W("vector", "gz", sch.cnt["gz"])
                    E("vector", lambda e, gt=gt, dmh=dmh: e.tensor_tensor(
                        out=gacc[gt][:, dmh * 512:(dmh + 1) * 512],
                        in0=gacc[gt][:, dmh * 512:(dmh + 1) * 512],
                        in1=ev_scr[:, :], op=ALU.add), inc="ga", drain=True)
                    sch.guard_add(("ev_scr",), "ga", sch.cnt["ga"])
            if eg == NEG - 1:
                for tb in range(MTB):
                    gt = mc * MTB + tb
                    gp = gt % 2
                    W("scalar", "ga", sch.cnt["ga"])
                    sch.guard_wait("scalar", ("g_sil", gp))
                    E("scalar", lambda e, gt=gt, gp=gp: e.activation(out=g_sil[gp][:, :],
                                                                     in_=gacc[gt][:, :],
                                                                     func=AF.Silu), inc="s", drain=True)
                    sch.guard_add(("gacc", gt), "s", sch.cnt["s"])
                    W("sync", "s", sch.cnt["s"])
                    row0 = th * TH + gt * 128
                    sch.dma("sync", g_o[row0:row0 + 128, :], g_sil[gp][:, :], "o_g")
                    sch.guard_add(("g_sil", gp), "o_g", None)

        def emit_scan_unit(ci, dgi, sidx):
            par = sidx % 2
            c0 = ci * TCH
            for j in range(DG):
                dt = dgi * DG + j
                sch.guard_wait("sync", ("delta_t", par, j))
                W("sync", "dsw", DSW_ALL)
                sch.dma("sync", delta_t[par][j][:, :],
                        delta_dr[dt * 128:(dt + 1) * 128, c0:c0 + TCH], f"sd{par}")
                sch.guard_wait("sync", ("u_t", par, j))
                sch.dma("sync", u_t[par][j][:, :],
                        ut_in[dt * 128:(dt + 1) * 128, c0:c0 + TCH], f"sd{par}")
            SD = sch.cnt[f"sd{par}"]
            for j in range(DG):
                W("vector", f"sd{par}", SD)
                sch.guard_wait("vector", ("du_t", j))
                E("vector", lambda e, j=j, par=par: e.tensor_tensor(
                    out=du_t[j][:, :], in0=delta_t[par][j][:, :], in1=u_t[par][j][:, :],
                    op=ALU.mult), inc="v", drain=True)
            se_last = [0] * DG
            for n in range(N_ST):
                bpar = n % 2
                sch.guard_wait("gpsimd", ("bbc", bpar))
                sch.dma("gpsimd", bbc[bpar][:, :],
                        dbct_in[64 + n:65 + n, c0:c0 + TCH].partition_broadcast(128), f"bc{bpar}")
                sch.guard_wait("gpsimd", ("cbc", bpar))
                sch.dma("gpsimd", cbc[bpar][:, :],
                        dbct_in[80 + n:81 + n, c0:c0 + TCH].partition_broadcast(128), f"bc{bpar}")
                BC = sch.cnt[f"bc{bpar}"]
                for j in range(DG):
                    dt = dgi * DG + j
                    apar = (n * DG + j) % 2
                    W("scalar", f"sd{par}", SD)
                    W("scalar", "v", V_A)
                    sch.guard_wait("scalar", ("a_scr", apar))
                    E("scalar", lambda e, j=j, dt=dt, n=n, par=par, apar=apar: e.activation(
                        out=a_scr[apar][:, :], in_=delta_t[par][j][:, :], func=AF.Exp,
                        scale=a_t[dt][:, n:n + 1]), inc="se", drain=True)
                    SE = sch.cnt["se"]
                    W("vector", f"bc{bpar}", BC)
                    sch.guard_wait("vector", ("b_scr",))
                    E("vector", lambda e, j=j, bpar=bpar: e.tensor_tensor(
                        out=b_scr[:, :], in0=du_t[j][:, :], in1=bbc[bpar][:, :], op=ALU.mult),
                      drain=True)
                    W("vector", "se", SE)
                    sch.guard_wait("vector", ("h_scr",))
                    E("vector", lambda e, dt=dt, n=n, apar=apar: e.tensor_tensor_scan(
                        out=h_scr[:, :], data0=a_scr[apar][:, :], data1=b_scr[:, :],
                        initial=state[dt][:, n:n + 1], op0=ALU.mult, op1=ALU.add),
                      inc="v", drain=True)
                    sch.guard_add(("a_scr", apar), "v", sch.cnt["v"])
                    sch.guard_add(("b_scr",), "v", sch.cnt["v"])
                    E("vector", lambda e, dt=dt, n=n: e.tensor_copy(
                        state[dt][:, n:n + 1], h_scr[:, TCH - 1:TCH]), drain=True)
                    if n == 0:
                        E("vector", lambda e, j=j, bpar=bpar: e.tensor_tensor(
                            out=acc_a[j][:, :], in0=h_scr[:, :], in1=cbc[bpar][:, :],
                            op=ALU.mult), inc="v", drain=True)
                    elif n == 1:
                        E("vector", lambda e, j=j, bpar=bpar: e.tensor_tensor(
                            out=acc_b[j][:, :], in0=h_scr[:, :], in1=cbc[bpar][:, :],
                            op=ALU.mult), inc="v", drain=True)
                    else:
                        E("vector", lambda e, j=j, bpar=bpar: e.tensor_tensor(
                            out=p_scr2[:, :], in0=h_scr[:, :], in1=cbc[bpar][:, :],
                            op=ALU.mult), drain=True)
                        acc = acc_a if n % 2 == 0 else acc_b
                        E("vector", lambda e, j=j, acc=acc: e.tensor_tensor(
                            out=acc[j][:, :], in0=acc[j][:, :], in1=p_scr2[:, :],
                            op=ALU.add), inc="v", drain=True)
                    sch.guard_add(("h_scr",), "v", sch.cnt["v"])
                sch.guard_add(("bbc", bpar), "v", sch.cnt["v"])
                sch.guard_add(("cbc", bpar), "v", sch.cnt["v"])
            for j in range(DG):
                dt = dgi * DG + j
                zpar = (sidx * DG + j) % 2
                sch.guard_wait("gpsimd", ("z_t", zpar))
                sch.dma("gpsimd", z_t[zpar][:, :], zt_in[dt * 128:(dt + 1) * 128, c0:c0 + TCH],
                        f"zl{zpar}")
                ZL = sch.cnt[f"zl{zpar}"]
                sch.guard_wait("vector", ("y_scr",))
                E("vector", lambda e, j=j, dt=dt, par=par: e.scalar_tensor_tensor(
                    out=y_scr[:, :], in0=u_t[par][j][:, :], scalar=dcol[:, dt:dt + 1],
                    in1=acc_a[j][:, :], op0=ALU.mult, op1=ALU.add), drain=True)
                sch.guard_wait("vector", ("y2_scr",))
                E("vector", lambda e, j=j: e.tensor_tensor(
                    out=y2_scr[:, :], in0=y_scr[:, :], in1=acc_b[j][:, :], op=ALU.add),
                  inc="v", drain=True)
                V_Y = sch.cnt["v"]
                sch.guard_add(("u_t", par, j), "v", V_Y)
                sch.guard_add(("y_scr",), "v", V_Y)
                sch.guard_add(("delta_t", par, j), "se", se_last[j])
                sch.guard_add(("delta_t", par, j), "v", V_Y)
                W("scalar", f"zl{zpar}", ZL)
                sch.guard_wait("scalar", ("sz_t",))
                E("scalar", lambda e, zpar=zpar: e.activation(out=sz_t[:, :], in_=z_t[zpar][:, :],
                                                              func=AF.Silu), inc="se", drain=True)
                sch.guard_add(("z_t", zpar), "se", sch.cnt["se"])
                W("vector", "se", sch.cnt["se"])
                ypar = (sidx * DG + j) % 2
                sch.guard_wait("vector", ("yz_t", ypar))
                E("vector", lambda e, ypar=ypar: e.tensor_tensor(
                    out=yz_t[ypar][:, :], in0=y2_scr[:, :], in1=sz_t[:, :], op=ALU.mult),
                  inc="v", drain=True)
                sch.guard_add(("y2_scr",), "v", sch.cnt["v"])
                sch.guard_add(("sz_t",), "v", sch.cnt["v"])
                W("sync", "v", sch.cnt["v"])
                sch.dma("sync", yzt_o[dt * 128:(dt + 1) * 128, c0:c0 + TCH], yz_t[ypar][:, :], "o_yz")
                sch.guard_add(("yz_t", ypar), "o_yz", None)

        sidx = 0
        midx = 0
        scan_q = list(scan_units)
        for th in range(NTH):
            emit_router_preamble(th)
            if th == 0:
                load_we_pair(0, 0, 0)
            units_this_th = [(th, eg, mc) for eg in range(NEG) for mc in range(NMC)]
            per = max(1, len(scan_q) // (NTH * len(units_this_th)) if NTH * len(units_this_th) else 1)
            n_scan_per_moe = len(scan_units) / max(1, len(moe_units))
            scan_credit = 0.0
            for u in units_this_th:
                emit_moe_unit(*u, midx)
                midx += 1
                scan_credit += len(scan_units) / len(moe_units)
                while scan_credit >= 1.0 and scan_q:
                    ci, dgi = scan_q.pop(0)
                    emit_scan_unit(ci, dgi, sidx)
                    sidx += 1
                    scan_credit -= 1.0
            for t in range(TH // 128):
                sch.guard_add(("xnt", t), "v", sch.cnt["v"])
        while scan_q:
            ci, dgi = scan_q.pop(0)
            emit_scan_unit(ci, dgi, sidx)
            sidx += 1

        sch.final_waits("sync")
        sch.run()
    return nc


NK = DIN // 128


def build_l3(T3):
    NTB = T3 // 128
    nc = bass.Bass("TRN2", target_bir_lowering=False)
    yzt_in = nc.dram_tensor("yzt_in", [DIN, T3], BF16, kind="ExternalInput").ap()
    wo_in = nc.dram_tensor("wo_in", [DIN, DM], F32, kind="ExternalInput").ap()
    x_in = nc.dram_tensor("x_in", [T3, DM], F32, kind="ExternalInput").ap()
    mrs_in = nc.dram_tensor("mrs_in", [128, 2 * NTB], F32, kind="ExternalInput").ap()
    g_in = nc.dram_tensor("g_in", [T3, DM], BF16, kind="ExternalInput").ap()
    out_o = nc.dram_tensor("out_o", [T3, DM], F32, kind="ExternalOutput").ap()

    from contextlib import ExitStack
    with ExitStack() as st:
        def sb(name, shape, dt):
            return st.enter_context(nc.sbuf_tensor(name, shape, dt))

        def psum(name, shape, dt):
            return st.enter_context(nc.psum_tensor(name, shape, dt))

        wo_bf = [sb(f"wo{k}", [128, DM], BF16) for k in range(NK)]
        yzt = [[sb(f"yzt{i}_{k}", [128, 128], BF16) for k in range(NK)] for i in range(2)]
        x_t = [sb(f"x3_{i}", [128, DM], F32) for i in range(2)]
        g_t = [sb(f"g3_{i}", [128, DM], BF16) for i in range(2)]
        mrs = sb("mrs3", [128, 2 * NTB], F32)
        xn_t = sb("xn3", [128, DM], F32)
        s_t = sb("s3", [128, DM], F32)
        sg_t = sb("sg3", [128, DM], F32)
        o_t = [sb(f"o3_{i}", [128, DM], F32) for i in range(2)]
        nmr = sb("nmr3", [128, 1], F32)
        ps_s = [psum(f"ps_s{i}", [128, 512], F32) for i in range(2)]

        sch = Sched(nc, st)
        E = sch.emit
        W = sch.wait

        for k in range(NK):
            sch.dma("gpsimd", wo_bf[k][:], wo_in[k * 128:(k + 1) * 128, :], "w")
        sch.dma("gpsimd", mrs[:, :], mrs_in[:, :], "w")
        W_ALL = sch.cnt["w"]
        W("tensor", "w", W_ALL)
        W("vector", "w", W_ALL)
        W("scalar", "w", W_ALL)

        for tb in range(NTB):
            par = tb % 2
            for k in range(NK):
                sch.guard_wait("sync", ("yzt", par, k))
                sch.dma("sync", yzt[par][k][:, :], yzt_in[k * 128:(k + 1) * 128,
                                                          tb * 128:(tb + 1) * 128], f"ld{par}")
            sch.guard_wait("sync", ("x_t", par))
            sch.dma("sync", x_t[par][:, :], x_in[tb * 128:(tb + 1) * 128, :], f"ld{par}")
            sch.guard_wait("sync", ("g_t", par))
            sch.dma("sync", g_t[par][:, :], g_in[tb * 128:(tb + 1) * 128, :], f"ld{par}")
            LD = sch.cnt[f"ld{par}"]
            for dmh in range(2):
                W("tensor", f"ld{par}", LD)
                sch.guard_wait("tensor", ("ps_s", dmh))
                for k in range(NK):
                    E("tensor", lambda e, k=k, dmh=dmh, par=par: e.matmul(
                        out=ps_s[dmh][:, :], lhsT=yzt[par][k][:, :],
                        rhs=wo_bf[k][:, dmh * 512:(dmh + 1) * 512],
                        start=(k == 0), stop=(k == NK - 1)),
                      inc=("mm" if k == NK - 1 else None))
            MM = sch.cnt["mm"]
            for k in range(NK):
                sch.guard_add(("yzt", par, k), "mm", MM)
            W("vector", f"ld{par}", LD)
            sch.guard_wait("vector", ("xn_t",))
            E("vector", lambda e, tb=tb, par=par: e.tensor_scalar(
                out=xn_t[:, :], in0=x_t[par][:, :], scalar1=mrs[:, 2 * tb:2 * tb + 1],
                scalar2=mrs[:, 2 * tb + 1:2 * tb + 2], op0=ALU.subtract, op1=ALU.mult),
              inc="v", drain=True)
            W("vector", "mm", MM)
            sch.guard_wait("vector", ("s_t",))
            for dmh in range(2):
                E("vector", lambda e, dmh=dmh: e.tensor_tensor(
                    out=s_t[:, dmh * 512:(dmh + 1) * 512], in0=ps_s[dmh][:, :],
                    in1=xn_t[:, dmh * 512:(dmh + 1) * 512], op=ALU.add),
                  inc=("v" if dmh == 1 else None), drain=True)
            for dmh in range(2):
                sch.guard_add(("ps_s", dmh), "v", sch.cnt["v"])
            sch.guard_add(("xn_t",), "v", sch.cnt["v"])
            sch.guard_wait("vector", ("sg_t",))
            E("vector", lambda e, par=par: e.tensor_tensor(
                out=sg_t[:, :], in0=s_t[:, :], in1=g_t[par][:, :], op=ALU.mult), drain=True)
            sch.guard_add(("s_t",), "v", sch.cnt["v"])
            opar = tb % 2
            sch.guard_wait("vector", ("o_t", opar))
            E("vector", lambda e, par=par, opar=opar: e.tensor_tensor(
                out=o_t[opar][:, :], in0=sg_t[:, :], in1=x_t[par][:, :], op=ALU.add),
              inc="v", drain=True)
            V_O = sch.cnt["v"]
            sch.guard_add(("sg_t",), "v", V_O)
            sch.guard_add(("x_t", par), "v", V_O)
            sch.guard_add(("g_t", par), "v", V_O)
            W("sync", "v", V_O)
            sch.dma("sync", out_o[tb * 128:(tb + 1) * 128, :], o_t[opar][:, :], "o_out")
            sch.guard_add(("o_t", opar), "o_out", None)
        sch.final_waits("sync")
        sch.run()
    return nc


import ml_dtypes
from concourse.bass_utils import run_bass_kernel_spmd

_BF = ml_dtypes.bfloat16


def kernel(L0_attn_x, L0_attn_in_proj_w, L0_attn_conv_w, L0_attn_conv_b,
           L0_attn_x_proj_w, L0_attn_dt_proj_w, L0_attn_dt_proj_b, L0_attn_A_log,
           L0_attn_D, L0_attn_out_proj_w, L0_attn_ln_g, L0_attn_ln_b,
           L0_ffn_router_w, L0_ffn_expert_weights):
    x = np.ascontiguousarray(L0_attn_x)
    B, L, _ = x.shape
    TH_TOK = L // 2
    ident = np.eye(128).astype(_BF)

    nc1 = build_l1(TH_TOK, 512)
    in1 = []
    for cid in range(8):
        b, h = cid // 2, cid % 2
        prev = (np.zeros((512, DM), np.float32) if h == 0
                else x[b, TH_TOK - 512:TH_TOK])
        x_sl = np.ascontiguousarray(np.concatenate([prev, x[b, h * TH_TOK:(h + 1) * TH_TOK]], axis=0))
        in1.append({
            "x_in": x_sl.astype(np.float32, copy=False),
            "w_in": L0_attn_in_proj_w, "cw_in": L0_attn_conv_w, "cb_in": L0_attn_conv_b,
            "xpw_in": L0_attn_x_proj_w, "lng_in": L0_attn_ln_g, "lnb_in": L0_attn_ln_b,
            "id_in": ident,
        })
    r1 = run_bass_kernel_spmd(nc1, in1, core_ids=list(range(8)))
    o1 = r1.results

    nc2 = build_l2(L, TH_TOK, 512, 1024, 256, 4)
    in2 = []
    for cid in range(8):
        b, h2 = cid // 2, cid % 2
        ut_full = np.concatenate([np.asarray(o1[2 * b]["ut_o"]), np.asarray(o1[2 * b + 1]["ut_o"])], axis=1)
        zt_full = np.concatenate([np.asarray(o1[2 * b]["zt_o"]), np.asarray(o1[2 * b + 1]["zt_o"])], axis=1)
        dbct = np.concatenate([np.asarray(o1[2 * b]["dbc_o"]), np.asarray(o1[2 * b + 1]["dbc_o"])], axis=1)
        sl = slice(h2 * 1024, (h2 + 1) * 1024)
        in2.append({
            "ut_in": np.ascontiguousarray(ut_full[sl]),
            "zt_in": np.ascontiguousarray(zt_full[sl]),
            "dbct_in": np.ascontiguousarray(dbct),
            "xn_in": np.asarray(o1[cid]["xn_o"]),
            "alog_in": np.ascontiguousarray(L0_attn_A_log[sl]),
            "dsl_in": np.ascontiguousarray(L0_attn_D[sl]),
            "dtw_in": np.ascontiguousarray(L0_attn_dt_proj_w[:, sl]),
            "dtb_in": np.ascontiguousarray(L0_attn_dt_proj_b[sl]),
            "rw_in": L0_ffn_router_w,
            "ew_in": L0_ffn_expert_weights,
            "id_in": ident,
        })
    r2 = run_bass_kernel_spmd(nc2, in2, core_ids=list(range(8)))
    o2 = r2.results

    nc3 = build_l3(TH_TOK)
    in3 = []
    for cid in range(8):
        b, h = cid // 2, cid % 2
        tsl = slice(h * TH_TOK, (h + 1) * TH_TOK)
        yzt = np.concatenate([np.asarray(o2[2 * b]["yzt_o"])[:, tsl],
                              np.asarray(o2[2 * b + 1]["yzt_o"])[:, tsl]], axis=0)
        in3.append({
            "yzt_in": np.ascontiguousarray(yzt),
            "wo_in": L0_attn_out_proj_w,
            "x_in": np.ascontiguousarray(x[b, tsl]),
            "mrs_in": np.ascontiguousarray(np.asarray(o1[cid]["mrs_o"])[:, 8:]),
            "g_in": np.asarray(o2[cid]["g_o"]),
        })
    r3 = run_bass_kernel_spmd(nc3, in3, core_ids=list(range(8)))
    out = np.empty((B, L, DM), np.float32)
    for cid in range(8):
        b, h = cid // 2, cid % 2
        out[b, h * TH_TOK:(h + 1) * TH_TOK] = np.asarray(r3.results[cid]["out_o"])
    return out


# version 06
# speedup vs baseline: 1.2780x; 1.2780x over previous
import numpy as np
import concourse.bass as bass
import concourse.mybir as mybir

F32 = mybir.dt.float32
BF16 = mybir.dt.bfloat16
AF = mybir.ActivationFunctionType
ALU = mybir.AluOpType
AX = mybir.AxisListType

DM = 1024
DIN = 2048
EPS = 1e-5
KF = DM // 128
NU = DIN // 128
NCB = 2 * NU


class Sched:
    ENGINES = ("sync", "gpsimd", "vector", "scalar", "tensor")
    DRAINS = False

    def __init__(self, nc, st):
        self.nc = nc
        self.recs = {e: [] for e in self.ENGINES}
        self.sems = {}
        self.cnt = {}
        self.guards = {}
        self.st = st
        self._pending = {e: {} for e in self.ENGINES}

    def sem(self, name):
        if name not in self.sems:
            self.sems[name] = self.st.enter_context(self.nc.semaphore(name))
            self.cnt[name] = 0
        return self.sems[name]

    def wait(self, eng, sem_name, val):
        if val <= 0:
            return
        p = self._pending[eng]
        p[sem_name] = max(p.get(sem_name, 0), val)

    def guard_wait(self, eng, key):
        for sem_name, tgt in self.guards.pop(key, []):
            if tgt is None:
                tgt = self.cnt[sem_name]
            self.wait(eng, sem_name, tgt)

    def guard_add(self, key, sem_name, tgt):
        self.guards.setdefault(key, []).append((sem_name, tgt))

    def emit(self, eng, fn, inc=None, inc_by=1, drain=False):
        pend = self._pending[eng]
        self._pending[eng] = {}
        val = None
        if inc is not None:
            self.sem(inc)
            self.cnt[inc] += inc_by
            val = self.cnt[inc]
        rec = {"fn": fn, "pend": pend, "inc": inc, "inc_by": inc_by,
               "drain": drain, "ord_wait": None, "_val": val}
        if drain and not Sched.DRAINS:
            prev = self.recs[eng][-1] if self.recs[eng] else None
            if prev is not None and prev["fn"] is not None:
                if prev["inc"] is None:
                    prev["inc"] = f"_ord_{eng}"
                    prev["inc_by"] = 1
                    self.sem(prev["inc"])
                rec["ord_prev"] = prev
        self.recs[eng].append(rec)
        return val

    def dma(self, eng, out, in_, sem_name):
        return self.emit(eng, lambda e: e.dma_start(out=out, in_=in_), inc=sem_name, inc_by=16)

    def final_waits(self, eng="sync"):
        for s, v in sorted(self.cnt.items()):
            if s.startswith("o_") and v > 0:
                self.wait(eng, s, v)
        self.emit(eng, lambda e: None)

    def run(self):
        for eng in self.ENGINES:
            ordc = 0
            for rec in self.recs[eng]:
                if rec["inc"] == f"_ord_{eng}":
                    ordc += 1
                    rec["_ord_val"] = ordc
            for rec in self.recs[eng]:
                p = rec.get("ord_prev")
                if p is not None:
                    if p["inc"] == f"_ord_{eng}":
                        rec["ord_wait"] = (p["inc"], p["_ord_val"])
                    else:
                        rec["ord_wait"] = (p["inc"], p.get("_val", None))
        cnt2 = {k: 0 for k in self.cnt}
        for eng in self.ENGINES:
            pass
        blk = self.st.enter_context(self.nc.Block())

        def mk(eng_name):
            def body(e):
                for rec in self.recs[eng_name]:
                    if rec["drain"] and Sched.DRAINS:
                        e.drain()
                    ow = rec.get("ord_wait")
                    if ow is not None and ow[1] is not None:
                        e.wait_ge(self.sems[ow[0]], ow[1])
                    for s, v in rec["pend"].items():
                        e.wait_ge(self.sems[s], v)
                    ins = rec["fn"](e)
                    if rec["inc"] is not None:
                        assert ins is not None, "inc requires a real instruction"
                        iv = rec["inc_by"]
                        ins.then_inc(self.sems[rec["inc"]], iv)
            return body
        blk.sync(mk("sync"))
        blk.gpsimd(mk("gpsimd"))
        blk.vector(mk("vector"))
        blk.scalar(mk("scalar"))
        blk.tensor(mk("tensor"))


def build_l1(T, CH=512):
    OV = CH
    T_PROC = OV + T
    assert T_PROC % CH == 0 and CH % 128 == 0
    NC = T_PROC // CH
    NTB = CH // 128

    nc = bass.Bass("TRN2", target_bir_lowering=False)
    x_in = nc.dram_tensor("x_in", [T_PROC, DM], F32, kind="ExternalInput").ap()
    w_in = nc.dram_tensor("w_in", [DM, 2 * DIN], F32, kind="ExternalInput").ap()
    cw_in = nc.dram_tensor("cw_in", [DIN, 4], F32, kind="ExternalInput").ap()
    cb_in = nc.dram_tensor("cb_in", [DIN], F32, kind="ExternalInput").ap()
    xpw_in = nc.dram_tensor("xpw_in", [DIN, 96], F32, kind="ExternalInput").ap()
    lng_in = nc.dram_tensor("lng_in", [DM], F32, kind="ExternalInput").ap()
    lnb_in = nc.dram_tensor("lnb_in", [DM], F32, kind="ExternalInput").ap()
    id_in = nc.dram_tensor("id_in", [128, 128], BF16, kind="ExternalInput").ap()

    ut_o = nc.dram_tensor("ut_o", [DIN, T], BF16, kind="ExternalOutput").ap()
    zt_o = nc.dram_tensor("zt_o", [DIN, T], BF16, kind="ExternalOutput").ap()
    xn_o = nc.dram_tensor("xn_o", [T, DM], BF16, kind="ExternalOutput").ap()
    dbc_o = nc.dram_tensor("dbc_o", [96, T], F32, kind="ExternalOutput").ap()
    mrs_o = nc.dram_tensor("mrs_o", [128, 2 * NC * NTB], F32, kind="ExternalOutput").ap()

    from contextlib import ExitStack
    with ExitStack() as st:
        def sb(name, shape, dt):
            return st.enter_context(nc.sbuf_tensor(name, shape, dt))

        def psum(name, shape, dt):
            return st.enter_context(nc.psum_tensor(name, shape, dt))

        w_bf = [sb(f"w_bf{k}", [128, 2 * DIN], BF16) for k in range(KF)]
        xpw_bf = [sb(f"xpw{k}", [128, 96], BF16) for k in range(NU)]
        cw_sb = sb("cw_sb", [128, 4 * NU], F32)
        cb_sb = sb("cb_sb", [128, NU], F32)
        g_bc = sb("g_bc", [128, DM], BF16)
        b_bc = sb("b_bc", [128, DM], BF16)
        lng_row = sb("lng_row", [1, DM], F32)
        lnb_row = sb("lnb_row", [1, DM], F32)
        id_bf = sb("id_bf", [128, 128], BF16)
        x_t = [sb(f"x_t{i}", [128, DM * NTB], F32) for i in range(2)]
        sq_scr = sb("sq_scr", [128, DM], F32)
        xn_scr = sb("xn_scr", [128, DM], F32)
        xn_bf = [sb(f"xn_bf{i}", [128, DM], BF16) for i in range(2)]
        xn2a = sb("xn2a", [128, DM], F32)
        xn2g = sb("xn2g", [128, DM], F32)
        xn2_bf = [sb(f"xn2_bf{i}", [128, DM], BF16) for i in range(2)]
        xn2t = [[sb(f"xn2t{i}_{k}", [128, CH], BF16) for k in range(KF)] for i in range(2)]
        u_scr = [sb(f"u_scr{i}", [128, CH + 3], F32) for i in range(2)]
        p_scr = [sb(f"p_scr{i}", [128, CH], F32) for i in range(2)]
        tails = sb("tails", [128, 3 * NU], F32)
        u_out = [[sb(f"u_out{i}_{u}", [128, CH], BF16) for u in range(NU)] for i in range(2)]
        z_bf = [sb(f"z_bf{i}", [128, CH], BF16) for i in range(4)]
        dbc_sb = [sb(f"dbc_sb{i}", [96, CH], F32) for i in range(2)]
        mrs_sb = sb("mrs_sb", [128, 2 * NC * NTB], F32)
        S = {}
        for nme in ["r1", "ss", "m1", "v1", "msq", "var", "lnv", "rstd1",
                    "v2a", "v2b", "v2c", "l2", "rstd2", "nmr", "sc12", "nms"]:
            S[nme] = sb(nme, [128, 1], F32)
        consts = sb("consts", [128, 4], F32)
        C_RDM = consts[:, 0:1]
        C_EPS = consts[:, 1:2]
        C_NEG1 = consts[:, 2:3]
        C_NEGH = consts[:, 3:4]
        nc.const_aps.register(F32, EPS, C_EPS)
        nc.const_aps.register(F32, -0.5, C_NEGH)
        nc.const_aps.register(F32, -1.0, C_NEG1)
        nc.const_aps.register(F32, 1.0 / DM, C_RDM)

        ps_tp = [psum(f"ps_tp{i}", [128, DM], BF16) for i in range(2)]
        ps_io = [psum(f"ps_io{i}", [128, CH], F32) for i in range(2)]
        ps_xp = psum("ps_xp", [96, CH], F32)

        sch = Sched(nc, st)
        for s in ["x0", "x1", "w", "wc", "v", "s", "t", "mm", "ec", "si", "cv", "db",
                  "o_u", "o_db", "o_z", "o_xn", "o_mrs"]:
            sch.sem(s)

        E = sch.emit
        W = sch.wait

        for k in range(KF):
            sch.dma("gpsimd", w_bf[k][:], w_in[k * 128:(k + 1) * 128, :], "w")
        for k in range(NU):
            sch.dma("gpsimd", xpw_bf[k][:], xpw_in[k * 128:(k + 1) * 128, :], "w")
        for ub in range(NU):
            sch.dma("gpsimd", cw_sb[:, 4 * ub:4 * ub + 4], cw_in[ub * 128:(ub + 1) * 128, :], "w")
        def _cb_load(e):
            return e.dma_start(out=cb_sb[:, :], in_=cb_in.rearrange("(a p) -> p a", p=128),
                               allow_slow_non_contiguous=True)
        sch.emit("gpsimd", _cb_load, inc="w", inc_by=16)
        sch.dma("gpsimd", lng_row[:], lng_in[None, :], "w")
        sch.dma("gpsimd", lnb_row[:], lnb_in[None, :], "w")
        sch.dma("gpsimd", id_bf[:], id_in[:, :], "w")
        sch.emit("gpsimd", lambda e: e.memset(consts[:, 0:1], 1.0 / DM), inc="wc")
        sch.emit("gpsimd", lambda e: e.memset(consts[:, 1:2], EPS), inc="wc")
        sch.emit("gpsimd", lambda e: e.memset(consts[:, 2:3], -1.0), inc="wc")
        sch.emit("gpsimd", lambda e: e.memset(consts[:, 3:4], -0.5), inc="wc")
        sch.emit("gpsimd", lambda e: e.memset(tails[:, :], 0.0), inc="wc")
        sch.dma("gpsimd", g_bc[:], lng_in[None, :].partition_broadcast(128), "w")
        sch.dma("gpsimd", b_bc[:], lnb_in[None, :].partition_broadcast(128), "w")
        W_ALL = sch.cnt["w"]
        W("vector", "w", W_ALL)
        W("vector", "wc", sch.cnt["wc"])
        W("scalar", "wc", sch.cnt["wc"])
        W("tensor", "wc", sch.cnt["wc"])
        V_BC = sch.cnt["v"]
        W("scalar", "w", W_ALL)
        W("tensor", "w", W_ALL)

        u_dma_idx = 0
        z_dma_idx = 0
        xn_dma_idx = 0
        db_dma_idx = 0

        def tb_work(c, tb):
            nonlocal xn_dma_idx
            g = c * NTB + tb
            xtb = x_t[c % 2][:, tb * DM:(tb + 1) * DM]
            xsem = f"x{c % 2}"
            xw = 16 * NTB * (c // 2 + 1)
            W("vector", xsem, xw)
            E("vector", lambda e: e.reduce_sum(S["r1"][:], xtb, axis=AX.X))
            W("scalar", xsem, xw)
            E("scalar", lambda e: e.activation(out=sq_scr[:], in_=xtb, func=AF.Square,
                                               accum_out=S["ss"][:]), inc="s")
            W("vector", "s", sch.cnt["s"])
            E("vector", lambda e: e.tensor_scalar(out=S["m1"][:], in0=S["r1"][:], scalar1=C_RDM, scalar2=None, op0=ALU.mult), drain=True)
            E("vector", lambda e: e.tensor_scalar(out=S["v1"][:], in0=S["ss"][:], scalar1=C_RDM, scalar2=None, op0=ALU.mult))
            E("vector", lambda e: e.scalar_tensor_tensor(out=S["var"][:], in0=S["m1"][:], scalar=S["m1"][:, 0:1], in1=S["v1"][:], op0=ALU.mult, op1=ALU.subtract), inc="v", drain=True)
            W("scalar", "v", sch.cnt["v"])
            E("scalar", lambda e: e.activation(out=S["lnv"][:], in_=S["var"][:], func=AF.Ln, scale=-1.0, bias=EPS))
            E("scalar", lambda e: e.activation(out=S["rstd1"][:], in_=S["lnv"][:], func=AF.Exp, scale=-0.5), inc="s", drain=True)
            W("vector", "s", sch.cnt["s"])
            E("vector", lambda e: e.scalar_tensor_tensor(out=S["v2b"][:], in0=S["rstd1"][:], scalar=S["rstd1"][:, 0:1], in1=S["var"][:], op0=ALU.mult, op1=ALU.mult), inc="v", drain=True)
            W("scalar", "v", sch.cnt["v"])
            E("scalar", lambda e: e.activation(out=S["l2"][:], in_=S["v2b"][:], func=AF.Ln, scale=-1.0, bias=EPS))
            E("scalar", lambda e: e.activation(out=S["rstd2"][:], in_=S["l2"][:], func=AF.Exp, scale=-0.5), inc="s", drain=True)
            W("vector", "s", sch.cnt["s"])
            E("vector", lambda e: e.scalar_tensor_tensor(out=S["nmr"][:], in0=S["m1"][:], scalar=C_NEG1[:, 0:1], in1=S["rstd1"][:], op0=ALU.mult, op1=ALU.mult), drain=True)
            E("vector", lambda e: e.tensor_tensor(out=S["sc12"][:], in0=S["rstd1"][:], in1=S["rstd2"][:], op=ALU.mult))
            E("vector", lambda e: e.scalar_tensor_tensor(out=S["nms"][:], in0=S["m1"][:], scalar=C_NEG1[:, 0:1], in1=S["sc12"][:], op0=ALU.mult, op1=ALU.mult), drain=True)
            E("vector", lambda e: e.tensor_copy(mrs_sb[:, 2 * g:2 * g + 1], S["m1"][:]))
            E("vector", lambda e: e.tensor_copy(mrs_sb[:, 2 * g + 1:2 * g + 2], S["rstd1"][:]), inc="v")
            W("scalar", "v", sch.cnt["v"])
            sch.guard_wait("scalar", ("xn_bf", g % 2))
            E("scalar", lambda e: e.activation(out=xn_scr[:], in_=xtb, func=AF.Identity,
                                               scale=S["rstd1"][:, 0:1], bias=S["nmr"][:, 0:1]))
            E("scalar", lambda e: e.activation(out=xn2a[:], in_=xtb, func=AF.Identity,
                                               scale=S["sc12"][:, 0:1], bias=S["nms"][:, 0:1]))
            E("scalar", lambda e: e.activation(out=xn_bf[g % 2][:], in_=xn_scr[:], func=AF.Copy), inc="s", drain=True)
            s_xn = sch.cnt["s"]
            if c >= 1:
                og = (c - 1) * NTB + tb
                W("gpsimd", "s", s_xn)
                sch.dma("gpsimd", xn_o[og * 128:(og + 1) * 128, :], xn_bf[g % 2][:, :], "o_xn")
                xn_dma_idx += 1
                sch.guard_add(("xn_bf", g % 2), "o_xn", None)
            W("vector", "s", s_xn)
            W("vector", "v", V_BC)
            sch.guard_wait("vector", ("xn2_bf", g % 2))
            E("vector", lambda e: e.tensor_tensor(out=xn2g[:], in0=xn2a[:], in1=g_bc[:], op=ALU.mult), drain=True)
            E("vector", lambda e: e.tensor_tensor(out=xn2_bf[g % 2][:], in0=xn2g[:], in1=b_bc[:], op=ALU.add), inc="v", drain=True)
            W("tensor", "v", sch.cnt["v"])
            sch.guard_wait("tensor", ("ps_tp", g % 2))
            for k in range(KF):
                def tp(e, k=k):
                    return e.transpose(out=ps_tp[g % 2][:, k * 128:(k + 1) * 128],
                                       in_=xn2_bf[g % 2][:, k * 128:(k + 1) * 128], identity=id_bf[:])
                E("tensor", tp, inc=("t" if k == KF - 1 else None))
            sch.guard_add(("xn2_bf", g % 2), "t", sch.cnt["t"])
            W("scalar", "t", sch.cnt["t"])
            sch.guard_wait("scalar", ("xn2t", c % 2, tb))
            for k in range(KF):
                def ev(e, k=k):
                    return e.activation(out=xn2t[c % 2][k][:, tb * 128:(tb + 1) * 128],
                                        in_=ps_tp[g % 2][:, k * 128:(k + 1) * 128], func=AF.Copy)
                E("scalar", ev, inc=("s" if k == KF - 1 else None))
            sch.guard_add(("ps_tp", g % 2), "s", sch.cnt["s"])
            return sch.cnt["s"]

        def chunk_mm(c, evac_milestones):
            nonlocal u_dma_idx, z_dma_idx, db_dma_idx
            W("tensor", "s", evac_milestones[c])
            zj_local = []
            for cb in range(NCB):
                gcb = c * NCB + cb
                sch.guard_wait("tensor", ("ps_io", gcb % 2))
                for k in range(KF):
                    def mm(e, k=k, cb=cb, gcb=gcb):
                        return e.matmul(out=ps_io[gcb % 2][:, :],
                                        lhsT=w_bf[k][:, cb * 128:(cb + 1) * 128],
                                        rhs=xn2t[c % 2][k][:, :],
                                        start=(k == 0), stop=(k == KF - 1))
                    E("tensor", mm, inc=("mm" if k == KF - 1 else None))
                mm_done = sch.cnt["mm"]
                W("scalar", "mm", mm_done)
                if cb < NU:
                    ub = cb
                    sch.guard_wait("scalar", ("u_scr", ub % 2))
                    E("scalar", lambda e, ub=ub, gcb=gcb: e.activation(
                        out=u_scr[ub % 2][:, 3:CH + 3], in_=ps_io[gcb % 2][:, :], func=AF.Copy), inc="ec")
                    sch.guard_add(("ps_io", gcb % 2), "ec", sch.cnt["ec"])
                    W("vector", "ec", sch.cnt["ec"])
                    sch.guard_wait("vector", ("p_scr", ub % 2))
                    w4 = cw_sb[:, 4 * ub:4 * ub + 4]
                    us = u_scr[ub % 2]
                    pp = p_scr[ub % 2]
                    E("vector", lambda e, ub=ub, us=us: e.tensor_copy(us[:, 0:3], tails[:, 3 * ub:3 * ub + 3]), drain=True)
                    E("vector", lambda e, us=us, pp=pp, w4=w4: e.tensor_scalar(
                        out=pp[:], in0=us[:, 0:CH], scalar1=w4[:, 0:1], scalar2=None, op0=ALU.mult), drain=True)
                    for j in (1, 2, 3):
                        E("vector", lambda e, us=us, pp=pp, w4=w4, j=j: e.scalar_tensor_tensor(
                            out=pp[:], in0=us[:, j:CH + j], scalar=w4[:, j:j + 1], in1=pp[:],
                            op0=ALU.mult, op1=ALU.add), drain=True)
                    E("vector", lambda e, ub=ub, us=us: e.tensor_copy(
                        tails[:, 3 * ub:3 * ub + 3], us[:, CH:CH + 3]), inc="cv", drain=True)
                    sch.guard_add(("u_scr", ub % 2), "cv", sch.cnt["cv"])
                    W("scalar", "cv", sch.cnt["cv"])
                    sch.guard_wait("scalar", ("u_out", c % 2, ub))
                    E("scalar", lambda e, c=c, ub=ub: e.activation(
                        out=u_out[c % 2][ub][:, :], in_=p_scr[ub % 2][:, :], func=AF.Silu,
                        bias=cb_sb[:, ub:ub + 1]), inc="si")
                    sch.guard_add(("p_scr", ub % 2), "si", sch.cnt["si"])
                    if c >= 1:
                        oc = c - 1
                        W("sync", "si", sch.cnt["si"])
                        sch.dma("sync", ut_o[ub * 128:(ub + 1) * 128, oc * CH:(oc + 1) * CH],
                                u_out[c % 2][ub][:, :], "o_u")
                        u_dma_idx += 1
                        sch.guard_add(("u_out", c % 2, ub), "o_u", None)
                else:
                    zb = z_dma_idx % 4
                    sch.guard_wait("scalar", ("z_bf", zb))
                    E("scalar", lambda e, zb=zb, gcb=gcb: e.activation(
                        out=z_bf[zb][:, :], in_=ps_io[gcb % 2][:, :], func=AF.Copy), inc="ec")
                    sch.guard_add(("ps_io", gcb % 2), "ec", sch.cnt["ec"])
                    if c >= 1:
                        oc = c - 1
                        ub = cb - NU
                        W("gpsimd", "ec", sch.cnt["ec"])
                        sch.dma("gpsimd", zt_o[ub * 128:(ub + 1) * 128, oc * CH:(oc + 1) * CH],
                                z_bf[zb][:, :], "o_z")
                        sch.guard_add(("z_bf", zb), "o_z", None)
                    z_dma_idx += 1
            W("tensor", "si", sch.cnt["si"])
            sch.guard_wait("tensor", ("ps_xp",))
            for k in range(NU):
                def mmx(e, k=k):
                    return e.matmul(out=ps_xp[:, :], lhsT=xpw_bf[k][:, :],
                                    rhs=u_out[c % 2][k][:, :],
                                    start=(k == 0), stop=(k == NU - 1))
                E("tensor", mmx, inc=("mm" if k == NU - 1 else None))
            for k in range(NU):
                sch.guard_add(("u_out", c % 2, k), "mm", sch.cnt["mm"])
            W("scalar", "mm", sch.cnt["mm"])
            sch.guard_wait("scalar", ("dbc_sb", c % 2))
            E("scalar", lambda e, c=c: e.activation(out=dbc_sb[c % 2][:, :], in_=ps_xp[:, :],
                                                    func=AF.Copy), inc="db")
            sch.guard_add(("ps_xp",), "db", sch.cnt["db"])
            if c >= 1:
                oc = c - 1
                W("sync", "db", sch.cnt["db"])
                sch.dma("sync", dbc_o[:, oc * CH:(oc + 1) * CH], dbc_sb[c % 2][:, :], "o_db")
                db_dma_idx += 1
                sch.guard_add(("dbc_sb", c % 2), "o_db", None)

        evac_ms = {}
        for c in range(NC):
            for tb in range(NTB):
                g = c * NTB + tb
                if c >= 2:
                    sch.guard_wait("sync", ("x_t", c % 2, tb))
                sch.dma("sync", x_t[c % 2][:, tb * DM:(tb + 1) * DM],
                        x_in[g * 128:(g + 1) * 128, :], f"x{c % 2}")
            for tb in range(NTB):
                ms = tb_work(c, tb)
                evac_ms[c] = ms
                g = c * NTB + tb
                sch.guard_add(("x_t", c % 2, tb), "s", sch.cnt["s"])
            chunk_mm(c, evac_ms)
        W("sync", "v", sch.cnt["v"])
        sch.dma("sync", mrs_o[:, :], mrs_sb[:, :], "o_mrs")
        sch.final_waits("sync")
        sch.run()
    return nc



N_ST = 16
N_EXP = 32


def build_l2(TB, TMOE, TCH=512, TH=2048, MC=256, DG=4):
    DH = 1024
    NDT = DH // 128
    NCHUNK = TB // TCH
    NTH = TMOE // TH
    NMC = TH // MC
    EG = 2
    NEG = N_EXP // EG
    NDG = NDT // DG
    MTB = MC // 128
    NSP = TB // 512
    assert TMOE % TH == 0 and TH % MC == 0 and MC % 128 == 0 and TB % TCH == 0
    assert TCH % 512 == 0 or TCH == 512

    nc = bass.Bass("TRN2", target_bir_lowering=False)
    ut_in = nc.dram_tensor("ut_in", [DH, TB], BF16, kind="ExternalInput").ap()
    zt_in = nc.dram_tensor("zt_in", [DH, TB], BF16, kind="ExternalInput").ap()
    dbct_in = nc.dram_tensor("dbct_in", [96, TB], F32, kind="ExternalInput").ap()
    xn_in = nc.dram_tensor("xn_in", [TMOE, DM], BF16, kind="ExternalInput").ap()
    alog_in = nc.dram_tensor("alog_in", [DH, N_ST], F32, kind="ExternalInput").ap()
    dsl_in = nc.dram_tensor("dsl_in", [DH], F32, kind="ExternalInput").ap()
    dtw_in = nc.dram_tensor("dtw_in", [64, DH], F32, kind="ExternalInput").ap()
    dtb_in = nc.dram_tensor("dtb_in", [DH], F32, kind="ExternalInput").ap()
    rw_in = nc.dram_tensor("rw_in", [N_EXP, DM], F32, kind="ExternalInput").ap()
    ew_in = nc.dram_tensor("ew_in", [N_EXP, DM, DM], F32, kind="ExternalInput").ap()
    id_in = nc.dram_tensor("id_in", [128, 128], BF16, kind="ExternalInput").ap()

    yzt_o = nc.dram_tensor("yzt_o", [DH, TB], BF16, kind="ExternalOutput").ap()
    g_o = nc.dram_tensor("g_o", [TMOE, DM], BF16, kind="ExternalOutput").ap()

    delta_dr = nc.dram_tensor("delta_dr", [DH, TB], BF16)
    bc_dr = nc.dram_tensor("bc_dr", [32, TB], BF16)
    wt_dr = nc.dram_tensor("wt_dr", [N_EXP, TMOE], BF16)

    from contextlib import ExitStack
    with ExitStack() as st:
        def sb(name, shape, dt):
            return st.enter_context(nc.sbuf_tensor(name, shape, dt))

        def psum(name, shape, dt):
            return st.enter_context(nc.psum_tensor(name, shape, dt))

        a_t = [sb(f"a_t{j}", [128, N_ST], F32) for j in range(NDT)]
        alog_scr = sb("alog_scr", [128, N_ST], F32)
        state = [sb(f"state{j}", [128, N_ST], F32) for j in range(NDT)]
        dcol = sb("dcol", [128, NDT], F32)
        dtbcol = sb("dtbcol", [128, NDT], F32)
        dtw_bf = sb("dtw_bf", [64, DH], BF16)
        dtT_st = [sb(f"dtT_st{i}", [64, 512], BF16) for i in range(2)]
        rw_bf = sb("rw_bf", [N_EXP, DM], BF16)
        rwt = [sb(f"rwt{k}", [128, N_EXP], BF16) for k in range(KF)]
        id_bf = sb("id_bf2", [128, 128], BF16)
        consts = sb("consts2", [128, 3], F32)
        C_NEG1 = consts[:, 0:1]
        C_BIGN = consts[:, 1:2]
        C_ONE = consts[:, 2:3]
        nc.const_aps.register(F32, -1.0, C_NEG1)
        nc.const_aps.register(F32, 1.0, C_ONE)

        delta_t = [[sb(f"del{i}_{j}", [128, TCH], BF16) for j in range(DG)] for i in range(2)]
        u_t = [[sb(f"u{i}_{j}", [128, TCH], BF16) for j in range(DG)] for i in range(2)]
        du_t = [sb(f"du_{j}", [128, TCH], BF16) for j in range(DG)]
        acc_a = [sb(f"acca{j}", [128, TCH], BF16) for j in range(DG)]
        acc_b = [sb(f"accb{j}", [128, TCH], BF16) for j in range(DG)]
        a_scr = [sb(f"a_scr{i}", [128, TCH], F32) for i in range(2)]
        b_scr = [sb(f"b_scr{i}", [128, TCH], BF16) for i in range(2)]
        h_scr = [sb(f"h_scr{i}", [128, TCH], BF16) for i in range(2)]
        p_scr2 = [sb(f"p_scr2_{i}", [128, TCH], BF16) for i in range(2)]
        y_scr = sb("y_scr", [128, TCH], F32)
        bbc = [sb(f"bbc{i}", [128, TCH], BF16) for i in range(2)]
        cbc = [sb(f"cbc{i}", [128, TCH], BF16) for i in range(2)]
        z_t = [sb(f"z_t{i}", [128, TCH], BF16) for i in range(2)]
        sz_t = sb("sz_t", [128, TCH], BF16)
        yz_t = [sb(f"yz_t{i}", [128, TCH], BF16) for i in range(2)]

        xn_tb = [sb(f"xn_tb{i}", [128, DM], BF16) for i in range(2)]
        xnt = [sb(f"xnt{k}", [128, TH], BF16) for k in range(KF)]
        gacc = [sb(f"gacc{t}", [128, DM], BF16) for t in range(TH // 128)]
        we_t = [sb(f"we_t{i}", [128, KF * DM], BF16) for i in range(4)]
        xw = [[sb(f"xw{i}_{k}", [128, MC], BF16) for k in range(KF)] for i in range(2)]
        wbc = [sb(f"wbc{i}", [128, MC], BF16) for i in range(2)]
        ev_scr = sb("ev_scr", [128, 512], BF16)
        g_sil = [sb(f"g_sil{i}", [128, DM], BF16) for i in range(2)]
        wt_tb = [sb(f"wt_tb{i}", [N_EXP, 128], BF16) for i in range(2)]
        wm_bf = sb("wm_bf", [128, N_EXP], BF16)
        Sr = {}
        for nme in ["lmax", "nlmax", "psr", "prcp", "m1v", "m2v"]:
            Sr[nme] = sb(nme, [128, 1], F32)
        pexp = sb("pexp", [128, N_EXP], F32)
        rw_t = sb("rw_t", [128, N_EXP], F32)
        eqm = sb("eqm", [128, N_EXP], F32)
        rw2 = sb("rw2", [128, N_EXP], F32)
        keep = sb("keep", [128, N_EXP], F32)
        sp_t3 = sb("sp_t3", [128, 512], F32)
        sp_t4 = sb("sp_t4", [128, 512], F32)
        del_sl = sb("del_sl", [128, 512], F32)
        a_scr.append(del_sl[:, 0:TCH])
        a_scr.append(sp_t4[:, 0:TCH])
        b_scr.append(sp_t3[:, :].bitcast(BF16)[:, 0:TCH])
        b_scr.append(sp_t3[:, :].bitcast(BF16)[:, 512:512 + TCH])

        ps_moe = [psum(f"ps_moe{i}", [128, 512], F32) for i in range(8)]
        ps_tp = ps_moe[0][:, :].bitcast(BF16)
        ps_wt = ps_moe[1][:, :].bitcast(BF16)[0:N_EXP, 0:128]
        ps_rt = ps_moe[2][:, 0:N_EXP]

        sch = Sched(nc, st)
        E = sch.emit
        W = sch.wait

        sch.dma("gpsimd", id_bf[:], id_in[:, :], "w")
        sch.dma("gpsimd", dtw_bf[:], dtw_in[:, :], "w")
        sch.dma("gpsimd", rw_bf[:], rw_in[:, :], "w")
        sch.dma("gpsimd", bc_dr[:, :], dbct_in[64:96, :], "wbcst")
        E("gpsimd", lambda e: e.dma_start(out=dcol[:, :], in_=dsl_in.rearrange("(a p) -> p a", p=128),
                                          allow_slow_non_contiguous=True), inc="w", inc_by=16)
        E("gpsimd", lambda e: e.dma_start(out=dtbcol[:, :], in_=dtb_in.rearrange("(a p) -> p a", p=128),
                                          allow_slow_non_contiguous=True), inc="w", inc_by=16)
        E("gpsimd", lambda e: e.memset(consts[:, 0:1], -1.0), inc="wc")
        E("gpsimd", lambda e: e.memset(consts[:, 1:2], -1e30), inc="wc")
        E("gpsimd", lambda e: e.memset(consts[:, 2:3], 1.0), inc="wc")
        for j in range(NDT):
            E("gpsimd", lambda e, j=j: e.memset(state[j][:, :], 0.0), inc="wc")
        for j in range(NDT):
            sch.guard_wait("gpsimd", ("alog_scr",))
            sch.dma("gpsimd", alog_scr[:, :], alog_in[j * 128:(j + 1) * 128, :], "wa")
            W("scalar", "wa", sch.cnt["wa"])
            E("scalar", lambda e, j=j: e.activation(out=a_t[j][:, :], in_=alog_scr[:, :],
                                                    func=AF.Exp), inc="s", drain=True)
            sch.guard_add(("alog_scr",), "s", sch.cnt["s"])
            W("vector", "s", sch.cnt["s"])
            E("vector", lambda e, j=j: e.tensor_scalar(out=a_t[j][:, :], in0=a_t[j][:, :],
                                                       scalar1=C_NEG1, scalar2=None, op0=ALU.mult),
              inc="v", drain=True)
        V_A = sch.cnt["v"]
        W("scalar", "wc", sch.cnt["wc"])
        W("scalar", "w", sch.cnt["w"])
        W("vector", "wc", sch.cnt["wc"])
        W("vector", "w", sch.cnt["w"])
        W("tensor", "w", sch.cnt["w"])
        for k in range(KF):
            if k > 0:
                sch.guard_wait("tensor", ("ps_moe", 0))
            E("tensor", lambda e, k=k: e.transpose(out=ps_tp[:, 0:N_EXP],
                                                   in_=rw_bf[:, k * 128:(k + 1) * 128],
                                                   identity=id_bf[0:N_EXP, 0:N_EXP]), inc="t")
            W("scalar", "t", sch.cnt["t"])
            E("scalar", lambda e, k=k: e.activation(out=rwt[k][:, :], in_=ps_tp[:, 0:N_EXP],
                                                    func=AF.Copy), inc="s")
            sch.guard_add(("ps_moe", 0), "s", sch.cnt["s"])

        dsp = 0
        for j in range(NDT):
            for ts in range(NSP):
                pd = ps_moe[dsp % 4]
                stp = dsp % 2
                sch.guard_wait("gpsimd", ("dtT_st", stp))
                sch.dma("gpsimd", dtT_st[stp][:, :], dbct_in[0:64, ts * 512:(ts + 1) * 512],
                        f"dts{stp}")
                sch.guard_wait("tensor", ("ps_moe", dsp % 4))
                W("tensor", "v", V_A)
                W("tensor", f"dts{stp}", sch.cnt[f"dts{stp}"])
                E("tensor", lambda e, j=j, ts=ts, pd=pd, stp=stp: e.matmul(
                    out=pd[:, :], lhsT=dtw_bf[:, j * 128:(j + 1) * 128],
                    rhs=dtT_st[stp][:, :],
                    start=True, stop=True), inc="mm")
                sch.guard_add(("dtT_st", stp), "mm", sch.cnt["mm"])
                W("scalar", "mm", sch.cnt["mm"])
                sch.guard_wait("scalar", ("del_sl",))
                E("scalar", lambda e, j=j, pd=pd: e.activation(out=sp_t3[:, :], in_=pd[:, :],
                                                               func=AF.Abs, bias=dtbcol[:, j:j + 1]))
                E("scalar", lambda e: e.activation(out=sp_t3[:, :], in_=sp_t3[:, :],
                                                   func=AF.Exp, scale=-1.0), drain=True)
                E("scalar", lambda e: e.activation(out=sp_t3[:, :], in_=sp_t3[:, :],
                                                   func=AF.Ln, bias=1.0), drain=True)
                E("scalar", lambda e, j=j, pd=pd: e.activation(out=sp_t4[:, :], in_=pd[:, :],
                                                               func=AF.Relu, bias=dtbcol[:, j:j + 1]),
                  inc="s", drain=True)
                sch.guard_add(("ps_moe", dsp % 4), "s", sch.cnt["s"])
                W("vector", "s", sch.cnt["s"])
                E("vector", lambda e: e.tensor_tensor(out=del_sl[:, :], in0=sp_t3[:, :],
                                                      in1=sp_t4[:, :], op=ALU.add), inc="v", drain=True)
                W("gpsimd", "v", sch.cnt["v"])
                sch.dma("gpsimd", delta_dr[j * 128:(j + 1) * 128, ts * 512:(ts + 1) * 512],
                        del_sl[:, :], "dsw")
                sch.guard_add(("del_sl",), "dsw", None)
                dsp += 1
        DSW_ALL = sch.cnt["dsw"]
        sch.guard_add(("a_scr", 2), "dsw", None)
        sch.guard_add(("a_scr", 3), "s", sch.cnt["s"])
        sch.guard_add(("b_scr", 2), "v", sch.cnt["v"])
        sch.guard_add(("b_scr", 3), "v", sch.cnt["v"])

        uidx = 0
        scan_units = [(ci, dgi) for ci in range(NCHUNK) for dgi in range(NDG)]
        moe_units = [(th, eg, mc) for th in range(NTH) for eg in range(NEG) for mc in range(NMC)]
        n_scan = len(scan_units)
        n_moe = len(moe_units)

        def emit_router_preamble(th):
            for tb in range(TH // 128):
                gtb = th * (TH // 128) + tb
                par = gtb % 2
                sch.guard_wait("gpsimd", ("xn_tb", par))
                sch.dma("gpsimd", xn_tb[par][:, :],
                        xn_in[(th * TH + tb * 128):(th * TH + (tb + 1) * 128), :], f"xn{par}")
                W("tensor", f"xn{par}", sch.cnt[f"xn{par}"])
                sch.guard_wait("tensor", ("ps_moe", 0))
                for k in range(KF):
                    E("tensor", lambda e, k=k, par=par: e.transpose(
                        out=ps_tp[:, k * 128:(k + 1) * 128],
                        in_=xn_tb[par][:, k * 128:(k + 1) * 128], identity=id_bf[:]),
                      inc=("t" if k == KF - 1 else None))
                sch.guard_add(("xn_tb", par), "t", sch.cnt["t"])
                W("scalar", "t", sch.cnt["t"])
                sch.guard_wait("scalar", ("xnt", tb))
                for k in range(KF):
                    E("scalar", lambda e, k=k, tb=tb: e.activation(
                        out=xnt[k][:, tb * 128:(tb + 1) * 128],
                        in_=ps_tp[:, k * 128:(k + 1) * 128], func=AF.Copy),
                      inc=("s" if k == KF - 1 else None))
                sch.guard_add(("ps_moe", 0), "s", sch.cnt["s"])
                W("tensor", "s", sch.cnt["s"])
                sch.guard_wait("tensor", ("ps_moe", 2))
                for k in range(KF):
                    E("tensor", lambda e, k=k, tb=tb: e.matmul(
                        out=ps_rt[:, :], lhsT=xnt[k][:, tb * 128:(tb + 1) * 128],
                        rhs=rwt[k][:, :], start=(k == 0), stop=(k == KF - 1)),
                      inc=("t" if k == KF - 1 else None))
                T_RT = sch.cnt["t"]
                W("vector", "t", T_RT)
                E("vector", lambda e: e.reduce_max(Sr["lmax"][:], ps_rt[:, :], axis=AX.X), drain=True)
                E("vector", lambda e: e.tensor_scalar(out=Sr["nlmax"][:], in0=Sr["lmax"][:],
                                                      scalar1=C_NEG1, scalar2=None, op0=ALU.mult),
                  inc="v", drain=True)
                W("scalar", "v", sch.cnt["v"])
                E("scalar", lambda e: e.activation(out=pexp[:, :], in_=ps_rt[:, :], func=AF.Exp,
                                                   bias=Sr["nlmax"][:, 0:1]), inc="s", drain=True)
                sch.guard_add(("ps_moe", 2), "s", sch.cnt["s"])
                W("vector", "s", sch.cnt["s"])
                E("vector", lambda e: e.reduce_sum(Sr["psr"][:], pexp[:, :], axis=AX.X), drain=True)
                E("vector", lambda e: e.reciprocal(Sr["prcp"][:], Sr["psr"][:]), drain=True)
                E("vector", lambda e: e.tensor_scalar(out=rw_t[:, :], in0=pexp[:, :],
                                                      scalar1=Sr["prcp"][:, 0:1], scalar2=None,
                                                      op0=ALU.mult), drain=True)
                E("vector", lambda e: e.reduce_max(Sr["m1v"][:], rw_t[:, :], axis=AX.X), drain=True)
                E("vector", lambda e: e.tensor_scalar(out=eqm[:, :], in0=rw_t[:, :],
                                                      scalar1=Sr["m1v"][:, 0:1], scalar2=None,
                                                      op0=ALU.is_equal), drain=True)
                E("vector", lambda e: e.scalar_tensor_tensor(out=rw2[:, :], in0=eqm[:, :],
                                                             scalar=C_BIGN[:, 0:1], in1=rw_t[:, :],
                                                             op0=ALU.mult, op1=ALU.add), drain=True)
                E("vector", lambda e: e.reduce_max(Sr["m2v"][:], rw2[:, :], axis=AX.X), drain=True)
                E("vector", lambda e: e.tensor_scalar(out=keep[:, :], in0=rw_t[:, :],
                                                      scalar1=Sr["m2v"][:, 0:1], scalar2=None,
                                                      op0=ALU.is_ge), drain=True)
                sch.guard_wait("vector", ("wm_bf",))
                E("vector", lambda e: e.tensor_tensor(out=wm_bf[:, :], in0=rw_t[:, :],
                                                      in1=keep[:, :], op=ALU.mult), inc="v", drain=True)
                W("tensor", "v", sch.cnt["v"])
                sch.guard_wait("tensor", ("ps_moe", 1))
                E("tensor", lambda e: e.transpose(out=ps_wt, in_=wm_bf[:, :],
                                                  identity=id_bf[:]), inc="t")
                sch.guard_add(("wm_bf",), "t", sch.cnt["t"])
                W("scalar", "t", sch.cnt["t"])
                wpar2 = tb % 2
                sch.guard_wait("scalar", ("wt_tb", wpar2))
                E("scalar", lambda e, wpar2=wpar2: e.activation(out=wt_tb[wpar2][:, :],
                                                                in_=ps_wt, func=AF.Copy), inc="s")
                sch.guard_add(("ps_moe", 1), "s", sch.cnt["s"])
                W("sync", "s", sch.cnt["s"])
                sch.dma("sync", wt_dr[:, th * TH + tb * 128: th * TH + (tb + 1) * 128],
                        wt_tb[wpar2][:, :], "wts")
                sch.guard_add(("wt_tb", wpar2), "wts", None)
            for t in range(TH // 128):
                sch.guard_wait("gpsimd", ("gacc", t))
                E("gpsimd", lambda e, t=t: e.memset(gacc[t][:, :], 0.0), inc="gz")

        def load_we_pair(eg, par, th):
            for el in range(EG):
                ex = eg * EG + el
                slot = 2 * par + el
                sch.guard_wait("gpsimd", ("we", slot))
                sch.dma("gpsimd", we_t[slot][:].rearrange("p (k d) -> p k d", k=KF),
                        ew_in[ex].rearrange("(k p) d -> p k d", p=128), f"we{par}")

        def emit_moe_issue(th, eg, mc, sidx):
            par = sidx % 2
            upar = sidx % 2
            wpar = eg % 2
            if mc == 0:
                nxt = eg + 1
                if nxt < NEG:
                    load_we_pair(nxt, (eg + 1) % 2, th)
                elif th + 1 < NTH:
                    load_we_pair(0, (eg + 1) % 2, th + 1)
            for el in range(EG):
                ex = eg * EG + el
                bpar = (2 * sidx + el) % 2
                sch.guard_wait("scalar", ("wbc", bpar))
                W("scalar", "wts", sch.cnt["wts"])
                sch.dma("scalar", wbc[bpar][:, :],
                        wt_dr[ex:ex + 1, th * TH + mc * MC: th * TH + (mc + 1) * MC].partition_broadcast(128),
                        f"wb{bpar}")
                W("vector", f"wb{bpar}", sch.cnt[f"wb{bpar}"])
                W("vector", "wts", sch.cnt["wts"])
                xpar = (2 * sidx + el) % 2
                for k in range(KF):
                    sch.guard_wait("vector", ("xw", xpar, k))
                    E("vector", lambda e, k=k, xpar=xpar, bpar=bpar: e.tensor_tensor(
                        out=xw[xpar][k][:, :], in0=xnt[k][:, mc * MC:(mc + 1) * MC],
                        in1=wbc[bpar][:, :], op=ALU.mult),
                      inc=("v" if k == KF - 1 else None), drain=(k == 0))
                sch.guard_add(("wbc", bpar), "v", sch.cnt["v"])
                V_MASK = sch.cnt["v"]
                W("tensor", "v", V_MASK)
                W("tensor", f"we{wpar}", sch.cnt[f"we{wpar}"])
                for tb in range(MTB):
                    for dmh in range(2):
                        pj = 4 * upar + 2 * tb + dmh
                        if el == 0:
                            sch.guard_wait("tensor", ("ps_moe", pj))
                        for k in range(KF):
                            E("tensor", lambda e, k=k, tb=tb, dmh=dmh, el=el, pj=pj, xpar=xpar, wpar=wpar: e.matmul(
                                out=ps_moe[pj][:, :],
                                lhsT=xw[xpar][k][:, tb * 128:(tb + 1) * 128],
                                rhs=we_t[2 * wpar + el][:, k * DM + dmh * 512:k * DM + (dmh + 1) * 512],
                                start=(el == 0 and k == 0), stop=(el == EG - 1 and k == KF - 1)),
                              inc=("mm" if (el == EG - 1 and k == KF - 1) else None))
            MM_D = sch.cnt["mm"]
            if mc == NMC - 1:
                for el in range(EG):
                    sch.guard_add(("we", 2 * wpar + el), "mm", MM_D)
            for xp in range(2):
                for k in range(KF):
                    sch.guard_add(("xw", xp, k), "mm", MM_D)
            return MM_D

        def emit_moe_drain(th, eg, mc, MM_D, upar):
            for tb in range(MTB):
                gt = mc * MTB + tb
                for dmh in range(2):
                    pj = 4 * upar + 2 * tb + dmh
                    W("scalar", "mm", MM_D)
                    sch.guard_wait("scalar", ("ev_scr",))
                    E("scalar", lambda e, pj=pj: e.activation(out=ev_scr[:, :], in_=ps_moe[pj][:, :],
                                                              func=AF.Copy), inc="s")
                    sch.guard_add(("ps_moe", pj), "s", sch.cnt["s"])
                    W("vector", "s", sch.cnt["s"])
                    W("vector", "gz", sch.cnt["gz"])
                    E("vector", lambda e, gt=gt, dmh=dmh: e.tensor_tensor(
                        out=gacc[gt][:, dmh * 512:(dmh + 1) * 512],
                        in0=gacc[gt][:, dmh * 512:(dmh + 1) * 512],
                        in1=ev_scr[:, :], op=ALU.add), inc="ga", drain=True)
                    sch.guard_add(("ev_scr",), "ga", sch.cnt["ga"])
            if eg == NEG - 1:
                for tb in range(MTB):
                    gt = mc * MTB + tb
                    gp = gt % 2
                    W("scalar", "ga", sch.cnt["ga"])
                    sch.guard_wait("scalar", ("g_sil", gp))
                    E("scalar", lambda e, gt=gt, gp=gp: e.activation(out=g_sil[gp][:, :],
                                                                     in_=gacc[gt][:, :],
                                                                     func=AF.Exp, scale=-1.0), inc="s", drain=True)
                    W("vector", "s", sch.cnt["s"])
                    def _gadd(e, gp=gp):
                        with nc.allow_low_precision(reason="bf16 sigmoid compose"):
                            return e.tensor_scalar(out=g_sil[gp][:, :], in0=g_sil[gp][:, :],
                                                   scalar1=C_ONE, scalar2=None, op0=ALU.add)
                    E("vector", _gadd, drain=True)
                    def _grec(e, gp=gp):
                        with nc.allow_low_precision(reason="bf16 sigmoid compose"):
                            return e.reciprocal(g_sil[gp][:, :], g_sil[gp][:, :])
                    E("vector", _grec, drain=True)
                    E("vector", lambda e, gt=gt, gp=gp: e.tensor_tensor(
                        out=g_sil[gp][:, :], in0=g_sil[gp][:, :], in1=gacc[gt][:, :], op=ALU.mult),
                      inc="v", drain=True)
                    sch.guard_add(("gacc", gt), "v", sch.cnt["v"])
                    W("sync", "v", sch.cnt["v"])
                    row0 = th * TH + gt * 128
                    sch.dma("sync", g_o[row0:row0 + 128, :], g_sil[gp][:, :], "o_g")
                    sch.guard_add(("g_sil", gp), "o_g", None)

        def emit_scan_unit(ci, dgi, sidx):
            yield
            par = sidx % 2
            c0 = ci * TCH
            for j in range(DG):
                dt = dgi * DG + j
                sch.guard_wait("sync", ("delta_t", par, j))
                W("sync", "dsw", DSW_ALL)
                sch.dma("sync", delta_t[par][j][:, :],
                        delta_dr[dt * 128:(dt + 1) * 128, c0:c0 + TCH], f"sd{par}")
                sch.guard_wait("sync", ("u_t", par, j))
                sch.dma("sync", u_t[par][j][:, :],
                        ut_in[dt * 128:(dt + 1) * 128, c0:c0 + TCH], f"sd{par}")
            SD = sch.cnt[f"sd{par}"]
            for j in range(DG):
                W("vector", f"sd{par}", SD)
                sch.guard_wait("vector", ("du_t", j))
                E("vector", lambda e, j=j, par=par: e.tensor_tensor(
                    out=du_t[j][:, :], in0=delta_t[par][j][:, :], in1=u_t[par][j][:, :],
                    op=ALU.mult), inc="v", drain=True)
            se_last = [0] * DG
            for n in range(N_ST):
                bpar = n % 2
                sch.guard_wait("sync", ("bbc", bpar))
                W("sync", "wbcst", sch.cnt["wbcst"])
                sch.dma("sync", bbc[bpar][:, :],
                        bc_dr[n:n + 1, c0:c0 + TCH].partition_broadcast(128), f"bc{bpar}")
                sch.guard_wait("sync", ("cbc", bpar))
                sch.dma("sync", cbc[bpar][:, :],
                        bc_dr[16 + n:17 + n, c0:c0 + TCH].partition_broadcast(128), f"bc{bpar}")
                BC = sch.cnt[f"bc{bpar}"]
                SE_j = [0] * DG
                for j in range(DG):
                    dt = dgi * DG + j
                    W("scalar", f"sd{par}", SD)
                    W("scalar", "v", V_A)
                    sch.guard_wait("scalar", ("a_scr", j))
                    E("scalar", lambda e, j=j, dt=dt, n=n, par=par: e.activation(
                        out=a_scr[j][:, :], in_=delta_t[par][j][:, :], func=AF.Exp,
                        scale=a_t[dt][:, n:n + 1]), inc="se", drain=True)
                    SE_j[j] = sch.cnt["se"]
                    se_last[j] = SE_j[j]
                B_j = [0] * DG
                for j in range(DG):
                    W("vector", f"bc{bpar}", BC)
                    sch.guard_wait("vector", ("b_scr", j))
                    E("vector", lambda e, j=j, bpar=bpar: e.tensor_tensor(
                        out=b_scr[j][:, :], in0=du_t[j][:, :], in1=bbc[bpar][:, :], op=ALU.mult),
                      inc="v", drain=(j == 0))
                    B_j[j] = sch.cnt["v"]
                for j in range(DG):
                    dt = dgi * DG + j
                    hp = j % 2
                    W("vector", "se", SE_j[j])
                    W("vector", "v", B_j[j])
                    sch.guard_wait("vector", ("h_scr", hp))
                    E("vector", lambda e, dt=dt, n=n, j=j, hp=hp: e.tensor_tensor_scan(
                        out=h_scr[hp][:, :], data0=a_scr[j][:, :], data1=b_scr[j][:, :],
                        initial=state[dt][:, n:n + 1], op0=ALU.mult, op1=ALU.add),
                      inc="v")
                    sch.guard_add(("a_scr", j), "v", sch.cnt["v"])
                    sch.guard_add(("b_scr", j), "v", sch.cnt["v"])
                    E("vector", lambda e, dt=dt, n=n, hp=hp: e.tensor_copy(
                        state[dt][:, n:n + 1], h_scr[hp][:, TCH - 1:TCH]), drain=True)
                    if n == 0:
                        E("vector", lambda e, j=j, bpar=bpar, hp=hp: e.tensor_tensor(
                            out=acc_a[j][:, :], in0=h_scr[hp][:, :], in1=cbc[bpar][:, :],
                            op=ALU.mult), inc="v")
                    elif n == 1:
                        E("vector", lambda e, j=j, bpar=bpar, hp=hp: e.tensor_tensor(
                            out=acc_b[j][:, :], in0=h_scr[hp][:, :], in1=cbc[bpar][:, :],
                            op=ALU.mult), inc="v")
                    else:
                        pp2 = j % 2
                        sch.guard_wait("vector", ("p_scr2", pp2))
                        E("vector", lambda e, j=j, bpar=bpar, hp=hp, pp2=pp2: e.tensor_tensor(
                            out=p_scr2[pp2][:, :], in0=h_scr[hp][:, :], in1=cbc[bpar][:, :],
                            op=ALU.mult))
                        acc = acc_a if n % 2 == 0 else acc_b
                        E("vector", lambda e, j=j, acc=acc, pp2=pp2: e.tensor_tensor(
                            out=acc[j][:, :], in0=acc[j][:, :], in1=p_scr2[pp2][:, :],
                            op=ALU.add), inc="v", drain=True)
                        sch.guard_add(("p_scr2", pp2), "v", sch.cnt["v"])
                    sch.guard_add(("h_scr", hp), "v", sch.cnt["v"])
                sch.guard_add(("bbc", bpar), "v", sch.cnt["v"])
                sch.guard_add(("cbc", bpar), "v", sch.cnt["v"])
                yield
            for j in range(DG):
                dt = dgi * DG + j
                zpar = (sidx * DG + j) % 2
                sch.guard_wait("scalar", ("z_t", zpar))
                sch.dma("scalar", z_t[zpar][:, :], zt_in[dt * 128:(dt + 1) * 128, c0:c0 + TCH],
                        f"zl{zpar}")
                ZL = sch.cnt[f"zl{zpar}"]
                sch.guard_wait("vector", ("y_scr",))
                E("vector", lambda e, j=j, dt=dt, par=par: e.scalar_tensor_tensor(
                    out=y_scr[:, :], in0=u_t[par][j][:, :], scalar=dcol[:, dt:dt + 1],
                    in1=acc_a[j][:, :], op0=ALU.mult, op1=ALU.add), drain=True)
                E("vector", lambda e, j=j: e.tensor_tensor(
                    out=y_scr[:, :], in0=y_scr[:, :], in1=acc_b[j][:, :], op=ALU.add),
                  inc="v", drain=True)
                V_Y = sch.cnt["v"]
                sch.guard_add(("u_t", par, j), "v", V_Y)
                sch.guard_add(("y_scr",), "v", V_Y)
                sch.guard_add(("delta_t", par, j), "se", se_last[j])
                sch.guard_add(("delta_t", par, j), "v", V_Y)
                W("scalar", f"zl{zpar}", ZL)
                sch.guard_wait("scalar", ("sz_t",))
                E("scalar", lambda e, zpar=zpar: e.activation(out=sz_t[:, :], in_=z_t[zpar][:, :],
                                                              func=AF.Exp, scale=-1.0), inc="se", drain=True)
                W("vector", "se", sch.cnt["se"])
                def _szadd(e):
                    with nc.allow_low_precision(reason="bf16 sigmoid compose"):
                        return e.tensor_scalar(out=sz_t[:, :], in0=sz_t[:, :],
                                               scalar1=C_ONE, scalar2=None, op0=ALU.add)
                E("vector", _szadd, drain=True)
                def _szrec(e):
                    with nc.allow_low_precision(reason="bf16 sigmoid compose"):
                        return e.reciprocal(sz_t[:, :], sz_t[:, :])
                E("vector", _szrec, drain=True)
                E("vector", lambda e, zpar=zpar: e.tensor_tensor(
                    out=sz_t[:, :], in0=sz_t[:, :], in1=z_t[zpar][:, :], op=ALU.mult),
                  inc="v", drain=True)
                sch.guard_add(("z_t", zpar), "v", sch.cnt["v"])
                ypar = (sidx * DG + j) % 2
                sch.guard_wait("vector", ("yz_t", ypar))
                E("vector", lambda e, ypar=ypar: e.tensor_tensor(
                    out=yz_t[ypar][:, :], in0=y_scr[:, :], in1=sz_t[:, :], op=ALU.mult),
                  inc="v", drain=True)
                sch.guard_add(("y_scr",), "v", sch.cnt["v"])
                sch.guard_add(("sz_t",), "v", sch.cnt["v"])
                W("sync", "v", sch.cnt["v"])
                sch.dma("sync", yzt_o[dt * 128:(dt + 1) * 128, c0:c0 + TCH], yz_t[ypar][:, :], "o_yz")
                sch.guard_add(("yz_t", ypar), "o_yz", None)

        sidx = 0
        midx = 0
        from collections import deque
        pendq = deque()
        scan_q = list(scan_units)
        scan_gen = [None]
        yields_per_moe = (len(scan_units) * (N_ST + 1)) / max(1, len(moe_units))

        def advance_scan(k):
            nonlocal sidx
            left = k
            while left > 0:
                if scan_gen[0] is None:
                    if not scan_q:
                        return
                    ci, dgi = scan_q.pop(0)
                    scan_gen[0] = emit_scan_unit(ci, dgi, sidx)
                    sidx += 1
                try:
                    next(scan_gen[0])
                    left -= 1
                except StopIteration:
                    scan_gen[0] = None

        scan_credit = 0.0
        for th in range(NTH):
            emit_router_preamble(th)
            if th == 0:
                load_we_pair(0, 0, 0)
            units_this_th = [(th, eg, mc) for eg in range(NEG) for mc in range(NMC)]
            for u in units_this_th:
                if len(pendq) >= 2:
                    emit_moe_drain(*pendq.popleft())
                mmd = emit_moe_issue(*u, midx)
                pendq.append((u[0], u[1], u[2], mmd, midx % 2))
                midx += 1
                scan_credit += yields_per_moe
                nadv = int(scan_credit)
                if nadv:
                    advance_scan(nadv)
                    scan_credit -= nadv
            while pendq:
                emit_moe_drain(*pendq.popleft())
            for t in range(TH // 128):
                sch.guard_add(("xnt", t), "v", sch.cnt["v"])
        advance_scan(10 ** 9)

        sch.final_waits("sync")
        sch.run()
    return nc


NK = DIN // 128


def build_l3(T3):
    NTB = T3 // 128
    nc = bass.Bass("TRN2", target_bir_lowering=False)
    yzt_in = nc.dram_tensor("yzt_in", [DIN, T3], BF16, kind="ExternalInput").ap()
    wo_in = nc.dram_tensor("wo_in", [DIN, DM], F32, kind="ExternalInput").ap()
    x_in = nc.dram_tensor("x_in", [T3, DM], F32, kind="ExternalInput").ap()
    mrs_in = nc.dram_tensor("mrs_in", [128, 2 * NTB], F32, kind="ExternalInput").ap()
    g_in = nc.dram_tensor("g_in", [T3, DM], BF16, kind="ExternalInput").ap()
    out_o = nc.dram_tensor("out_o", [T3, DM], F32, kind="ExternalOutput").ap()

    from contextlib import ExitStack
    with ExitStack() as st:
        def sb(name, shape, dt):
            return st.enter_context(nc.sbuf_tensor(name, shape, dt))

        def psum(name, shape, dt):
            return st.enter_context(nc.psum_tensor(name, shape, dt))

        wo_bf = [sb(f"wo{k}", [128, DM], BF16) for k in range(NK)]
        yzt = [[sb(f"yzt{i}_{k}", [128, 128], BF16) for k in range(NK)] for i in range(2)]
        x_t = [sb(f"x3_{i}", [128, DM], F32) for i in range(2)]
        g_t = [sb(f"g3_{i}", [128, DM], BF16) for i in range(2)]
        mrs = sb("mrs3", [128, 2 * NTB], F32)
        xn_t = sb("xn3", [128, DM], F32)
        s_t = sb("s3", [128, DM], F32)
        sg_t = sb("sg3", [128, DM], F32)
        o_t = [sb(f"o3_{i}", [128, DM], F32) for i in range(2)]
        nmr = sb("nmr3", [128, 1], F32)
        ps_s = [psum(f"ps_s{i}", [128, 512], F32) for i in range(2)]

        sch = Sched(nc, st)
        E = sch.emit
        W = sch.wait

        for k in range(NK):
            sch.dma("gpsimd", wo_bf[k][:], wo_in[k * 128:(k + 1) * 128, :], "w")
        sch.dma("gpsimd", mrs[:, :], mrs_in[:, :], "w")
        W_ALL = sch.cnt["w"]
        W("tensor", "w", W_ALL)
        W("vector", "w", W_ALL)
        W("scalar", "w", W_ALL)

        for tb in range(NTB):
            par = tb % 2
            for k in range(NK):
                sch.guard_wait("sync", ("yzt", par, k))
                sch.dma("sync", yzt[par][k][:, :], yzt_in[k * 128:(k + 1) * 128,
                                                          tb * 128:(tb + 1) * 128], f"ld{par}")
            sch.guard_wait("sync", ("x_t", par))
            sch.dma("sync", x_t[par][:, :], x_in[tb * 128:(tb + 1) * 128, :], f"ld{par}")
            sch.guard_wait("sync", ("g_t", par))
            sch.dma("sync", g_t[par][:, :], g_in[tb * 128:(tb + 1) * 128, :], f"ld{par}")
            LD = sch.cnt[f"ld{par}"]
            for dmh in range(2):
                W("tensor", f"ld{par}", LD)
                sch.guard_wait("tensor", ("ps_s", dmh))
                for k in range(NK):
                    E("tensor", lambda e, k=k, dmh=dmh, par=par: e.matmul(
                        out=ps_s[dmh][:, :], lhsT=yzt[par][k][:, :],
                        rhs=wo_bf[k][:, dmh * 512:(dmh + 1) * 512],
                        start=(k == 0), stop=(k == NK - 1)),
                      inc=("mm" if k == NK - 1 else None))
            MM = sch.cnt["mm"]
            for k in range(NK):
                sch.guard_add(("yzt", par, k), "mm", MM)
            W("vector", f"ld{par}", LD)
            sch.guard_wait("vector", ("xn_t",))
            E("vector", lambda e, tb=tb, par=par: e.tensor_scalar(
                out=xn_t[:, :], in0=x_t[par][:, :], scalar1=mrs[:, 2 * tb:2 * tb + 1],
                scalar2=mrs[:, 2 * tb + 1:2 * tb + 2], op0=ALU.subtract, op1=ALU.mult),
              inc="v", drain=True)
            W("vector", "mm", MM)
            sch.guard_wait("vector", ("s_t",))
            for dmh in range(2):
                E("vector", lambda e, dmh=dmh: e.tensor_tensor(
                    out=s_t[:, dmh * 512:(dmh + 1) * 512], in0=ps_s[dmh][:, :],
                    in1=xn_t[:, dmh * 512:(dmh + 1) * 512], op=ALU.add),
                  inc=("v" if dmh == 1 else None), drain=True)
            for dmh in range(2):
                sch.guard_add(("ps_s", dmh), "v", sch.cnt["v"])
            sch.guard_add(("xn_t",), "v", sch.cnt["v"])
            sch.guard_wait("vector", ("sg_t",))
            E("vector", lambda e, par=par: e.tensor_tensor(
                out=sg_t[:, :], in0=s_t[:, :], in1=g_t[par][:, :], op=ALU.mult), drain=True)
            sch.guard_add(("s_t",), "v", sch.cnt["v"])
            opar = tb % 2
            sch.guard_wait("vector", ("o_t", opar))
            E("vector", lambda e, par=par, opar=opar: e.tensor_tensor(
                out=o_t[opar][:, :], in0=sg_t[:, :], in1=x_t[par][:, :], op=ALU.add),
              inc="v", drain=True)
            V_O = sch.cnt["v"]
            sch.guard_add(("sg_t",), "v", V_O)
            sch.guard_add(("x_t", par), "v", V_O)
            sch.guard_add(("g_t", par), "v", V_O)
            W("sync", "v", V_O)
            sch.dma("sync", out_o[tb * 128:(tb + 1) * 128, :], o_t[opar][:, :], "o_out")
            sch.guard_add(("o_t", opar), "o_out", None)
        sch.final_waits("sync")
        sch.run()
    return nc


import ml_dtypes
from concourse.bass_utils import run_bass_kernel_spmd

_BF = ml_dtypes.bfloat16


def kernel(L0_attn_x, L0_attn_in_proj_w, L0_attn_conv_w, L0_attn_conv_b,
           L0_attn_x_proj_w, L0_attn_dt_proj_w, L0_attn_dt_proj_b, L0_attn_A_log,
           L0_attn_D, L0_attn_out_proj_w, L0_attn_ln_g, L0_attn_ln_b,
           L0_ffn_router_w, L0_ffn_expert_weights):
    x = np.ascontiguousarray(L0_attn_x)
    B, L, _ = x.shape
    TH_TOK = L // 2
    ident = np.eye(128).astype(_BF)

    nc1 = build_l1(TH_TOK, 512)
    in1 = []
    for cid in range(8):
        b, h = cid // 2, cid % 2
        prev = (np.zeros((512, DM), np.float32) if h == 0
                else x[b, TH_TOK - 512:TH_TOK])
        x_sl = np.ascontiguousarray(np.concatenate([prev, x[b, h * TH_TOK:(h + 1) * TH_TOK]], axis=0))
        in1.append({
            "x_in": x_sl.astype(np.float32, copy=False),
            "w_in": L0_attn_in_proj_w, "cw_in": L0_attn_conv_w, "cb_in": L0_attn_conv_b,
            "xpw_in": L0_attn_x_proj_w, "lng_in": L0_attn_ln_g, "lnb_in": L0_attn_ln_b,
            "id_in": ident,
        })
    r1 = run_bass_kernel_spmd(nc1, in1, core_ids=list(range(8)))
    o1 = r1.results

    nc2 = build_l2(L, TH_TOK, 512, 1024, 256, 4)
    in2 = []
    for cid in range(8):
        b, h2 = cid // 2, cid % 2
        ut_full = np.concatenate([np.asarray(o1[2 * b]["ut_o"]), np.asarray(o1[2 * b + 1]["ut_o"])], axis=1)
        zt_full = np.concatenate([np.asarray(o1[2 * b]["zt_o"]), np.asarray(o1[2 * b + 1]["zt_o"])], axis=1)
        dbct = np.concatenate([np.asarray(o1[2 * b]["dbc_o"]), np.asarray(o1[2 * b + 1]["dbc_o"])], axis=1)
        sl = slice(h2 * 1024, (h2 + 1) * 1024)
        in2.append({
            "ut_in": np.ascontiguousarray(ut_full[sl]),
            "zt_in": np.ascontiguousarray(zt_full[sl]),
            "dbct_in": np.ascontiguousarray(dbct),
            "xn_in": np.asarray(o1[cid]["xn_o"]),
            "alog_in": np.ascontiguousarray(L0_attn_A_log[sl]),
            "dsl_in": np.ascontiguousarray(L0_attn_D[sl]),
            "dtw_in": np.ascontiguousarray(L0_attn_dt_proj_w[:, sl]),
            "dtb_in": np.ascontiguousarray(L0_attn_dt_proj_b[sl]),
            "rw_in": L0_ffn_router_w,
            "ew_in": L0_ffn_expert_weights,
            "id_in": ident,
        })
    r2 = run_bass_kernel_spmd(nc2, in2, core_ids=list(range(8)))
    o2 = r2.results

    nc3 = build_l3(TH_TOK)
    in3 = []
    for cid in range(8):
        b, h = cid // 2, cid % 2
        tsl = slice(h * TH_TOK, (h + 1) * TH_TOK)
        yzt = np.concatenate([np.asarray(o2[2 * b]["yzt_o"])[:, tsl],
                              np.asarray(o2[2 * b + 1]["yzt_o"])[:, tsl]], axis=0)
        in3.append({
            "yzt_in": np.ascontiguousarray(yzt),
            "wo_in": L0_attn_out_proj_w,
            "x_in": np.ascontiguousarray(x[b, tsl]),
            "mrs_in": np.ascontiguousarray(np.asarray(o1[cid]["mrs_o"])[:, 8:]),
            "g_in": np.asarray(o2[cid]["g_o"]),
        })
    r3 = run_bass_kernel_spmd(nc3, in3, core_ids=list(range(8)))
    out = np.empty((B, L, DM), np.float32)
    for cid in range(8):
        b, h = cid // 2, cid % 2
        out[b, h * TH_TOK:(h + 1) * TH_TOK] = np.asarray(r3.results[cid]["out_o"])
    return out
